# version 12 (final)
# baseline (speedup 1.0000x reference)
import contextlib
import numpy as np
import ml_dtypes
import concourse.bass as bass
import concourse.mybir as mybir
from concourse.bass_utils import run_bass_kernel_spmd

F32 = mybir.dt.float32
BF16 = mybir.dt.bfloat16
AF = mybir.ActivationFunctionType
ALU = mybir.AluOpType
AX = mybir.AxisListType
BF = ml_dtypes.bfloat16

NCORE = 8
SEQ = 2
T = 2048
C = 1024
NCH = C // 128
NTT = T // 128
NH = 16
HD = 64
HF = 4096
NHCH = HF // 128
ER, ES = 8, 2
TOPK = 2
EPS_LN = 1e-5
EPS_RMS0 = float(np.finfo(np.float32).eps)
EPS_RMS1 = 1e-6
INV_D2 = 1.0 / (HD * HD)
_DBG = False


class _Call:
    __slots__ = ("meth", "a", "k", "incs")

    def __init__(self, meth, a, k):
        self.meth, self.a, self.k = meth, a, k
        self.incs = []

    def then_inc(self, sem, v):
        self.incs.append((sem, v))
        return self

    def replay(self, eng):
        ins = getattr(eng, self.meth)(*self.a, **self.k)
        for s, v in self.incs:
            ins = ins.then_inc(s, v)


class _EngRec:
    def __init__(self, prog, name):
        self._p, self._n = prog, name

    def __getattr__(self, meth):
        def call(*a, **k):
            c = _Call(meth, a, k)
            self._p.q[self._n].append(c)
            return c
        return call


class Prog:
    ENGS = ("sync", "tensor", "vector", "scalar", "gpsimd")

    def __init__(self, nc):
        self.nc = nc
        self.q = {e: [] for e in self.ENGS}
        for e in self.ENGS:
            setattr(self, e, _EngRec(self, e))

    def finish(self):
        q = self.q
        with self.nc.Block() as block:
            @block.sync
            def _(eng):
                for c in q["sync"]:
                    c.replay(eng)

            @block.tensor
            def _(eng):
                for c in q["tensor"]:
                    c.replay(eng)

            @block.vector
            def _(eng):
                for c in q["vector"]:
                    c.replay(eng)

            @block.scalar
            def _(eng):
                for c in q["scalar"]:
                    c.replay(eng)

            @block.gpsimd
            def _(eng):
                for c in q["gpsimd"]:
                    c.replay(eng)


class Counters:

    def __init__(self, nc, names):
        self.stack = contextlib.ExitStack()
        self.sem = {}
        self.n = {}
        for nm in names:
            self.sem[nm] = self.stack.enter_context(nc.semaphore(nm))
        self.n = {nm: 0 for nm in names}

    def inc(self, nm, v=1):
        self.n[nm] += v
        return self.sem[nm], self.n[nm]


def _host_prep(inputs):
    ip = {}
    x = np.ascontiguousarray(inputs["L0_attn_x"])
    ip["x"] = x.reshape(NCORE, SEQ, T, C)
    for nm, key in [("wq1t", "L0_attn_Wq1"), ("wk1t", "L0_attn_Wk1"),
                    ("wq2t", "L0_attn_Wq2"), ("wk2t", "L0_attn_Wk2"),
                    ("wvt", "L0_attn_Wv"), ("woutt", "L0_attn_Wout"),
                    ("wgt", "L1_attn_Wg"), ("wct", "L1_attn_Wc"),
                    ("w1t", "L1_attn_w1"), ("w2t", "L1_attn_w2"),
                    ("w3t", "L1_attn_w3"), ("wrt", "L1_ffn_W_r"),
                    ("wkt", "L1_ffn_W_k"), ("wvt2", "L1_ffn_W_v")]:
        ip[nm] = np.ascontiguousarray(inputs[key].T)
    for nm, key in [("g_s", "L0_ffn_g_shared"), ("g_r", "L0_ffn_g_routed"),
                    ("ln_w", "L1_attn_ln_w"), ("ffn_w", "L1_attn_ffn_w"),
                    ("mu_r", "L1_ffn_mu_r"), ("mu_k", "L1_ffn_mu_k")]:
        ip[nm] = np.ascontiguousarray(inputs[key].reshape(NCH, 128).T)
    ip["cent"] = np.ascontiguousarray(
        inputs["L0_ffn_centroids"].reshape(NCH, 128, ER).transpose(1, 0, 2)
        .reshape(128, NCH * ER))
    ip["w1s"] = np.ascontiguousarray(inputs["L0_ffn_W1_s"])
    ip["w2s"] = np.ascontiguousarray(inputs["L0_ffn_W2_s"])
    ip["w1r"] = np.ascontiguousarray(inputs["L0_ffn_W1_r"])
    ip["w2r"] = np.ascontiguousarray(inputs["L0_ffn_W2_r"])
    ip["b1s"] = np.ascontiguousarray(
        inputs["L0_ffn_b1_s"].reshape(ES, NHCH, 128).transpose(2, 0, 1)
        .reshape(128, ES * NHCH))
    ip["b1r"] = np.ascontiguousarray(
        inputs["L0_ffn_b1_r"].reshape(ER, NHCH, 128).transpose(2, 0, 1)
        .reshape(128, ER * NHCH))
    ip["b2s"] = np.ascontiguousarray(inputs["L0_ffn_b2_s"].reshape(1, ES * C))
    ip["b2r"] = np.ascontiguousarray(inputs["L0_ffn_b2_r"].reshape(1, ER * C))
    f = np.arange(0, HD, 2, dtype=np.float32) / HD
    inv_freq = (10000.0 ** -f)
    tt = np.arange(T, dtype=np.float32)
    freqs = tt[None, :] * inv_freq[:, None]
    cosb = np.cos(freqs).astype(BF).astype(np.float32)
    sinb = np.sin(freqs).astype(BF).astype(np.float32)
    cs64 = np.concatenate([cosb, cosb], 0)
    sn64 = np.concatenate([sinb, -sinb], 0)
    ip["rope_cs"] = np.ascontiguousarray(
        np.concatenate([cs64, cs64], 0).astype(BF))
    ip["rope_sn"] = np.ascontiguousarray(
        np.concatenate([sn64, sn64], 0).astype(BF))
    pm = np.zeros((128, 128), np.float32)
    for mm_ in range(128):
        pm[(mm_ % 64 + 32) % 64 + 64 * (mm_ // 64), mm_] = 1.0
    ip["perm_swap"] = np.ascontiguousarray(pm.astype(BF))
    kk = np.arange(128, dtype=np.int64)[:, None]
    qq = np.arange(512, dtype=np.int64)[None, :]
    masks = np.stack([(kk + 128 * j <= qq).astype(np.float32) * INV_D2
                      for j in range(4)])
    ip["masks"] = masks.astype(BF)
    ip["tri128"] = np.ascontiguousarray(
        (kk <= qq[:, 0:128]).astype(np.float32).astype(BF))
    ip["ident"] = np.eye(128, dtype=np.float32)
    ip["ones_row"] = np.ones((1, 128), dtype=np.float32)
    ip["ones_col"] = np.ones((128, 1), dtype=np.float32)
    return ip


def emit_l0_prep(nc, P, cn, dt):
    x = dt["x"]
    xn_fm, xnr_fm = dt["xn_fm"], dt["xnr_fm"]
    acc0, wg = dt["acc0"], dt["wg"]

    st = contextlib.ExitStack()
    sb = lambda nm, *a: st.enter_context(nc.sbuf_tensor("A_" + nm, *a))
    ps = lambda nm, *a: st.enter_context(nc.psum_tensor("A_" + nm, *a))
    xt = [sb(f"xt{b}", [128, C], F32) for b in range(2)]
    xn_tm = [sb(f"xntm{b}", [128, C], BF16) for b in range(2)]
    xnr_tm = [sb(f"xnrtm{b}", [128, C], BF16) for b in range(2)]
    acc_sb = [sb(f"accsb{b}", [128, C], F32) for b in range(2)]
    bs = sb("bs", [128, 12], F32)
    mv = sb("mv", [128, 2 * NTT], F32)
    stats = {nm: sb("st_" + nm, [128, NTT], F32)
             for nm in ("vp", "rvp", "re", "rre", "rstd", "r2", "c2", "rp1", "mr")}
    eps0_t = sb("eps0t", [128, 1], F32)
    idb = sb("idb", [128, 128], BF16)
    centb = sb("centb", [128, NCH * ER], BF16)
    xnf_sb = sb("xnf", [128, NCH * T], BF16)
    xnrf_sb = sb("xnrf", [128, NCH * T], BF16)
    sg = sb("sg", [128, ER], F32)
    gsc = {nm: sb("g_" + nm, [128, 1], F32)
           for nm in ("top1", "top2", "den", "rec")}
    eqm = sb("eqm", [128, ER], F32)
    smask = sb("smask", [128, ER], F32)
    gemask = sb("gemask", [128, ER], F32)
    wsc = sb("wsc", [128, ER], F32)
    wgt_sb = sb("wgt_sb", [128, ER], F32)
    pt = [ps(f"pt{b}", [128, 1024], BF16) for b in range(2)]
    pg = ps("pg", [128, ER], F32)

    SEM, N = cn.sem, cn.n
    P.sync.dma_start(out=idb[:], in_=dt["d_identb"][:]).then_inc(*_i(cn, "gc", 16))
    P.sync.dma_start(out=centb[:], in_=dt["d_centb"][:]).then_inc(*_i(cn, "gc", 16))
    gd_const = N["gc"]
    P.vector.memset(eps0_t[:], EPS_RMS0)

    dv_g = None
    gd_wg = None
    dv_done = {}
    pe_done = {}
    ac_evac = {}
    gd_acc = {}
    for s in range(SEQ):
        for i in range(NTT):
            g = s * NTT + i
            b = g % 2
            mvc = mv[:, 2 * i:2 * i + 2]
            mean = mv[:, 2 * i:2 * i + 1]
            var = mv[:, 2 * i + 1:2 * i + 2]
            col = lambda nm: stats[nm][:, i:i + 1]
            lds = "ld0" if b == 0 else "ld1"
            if g >= 2:
                P.sync.wait_ge(SEM["dv"], dv_done[g - 2])
            P.sync.dma_start(out=xt[b][:], in_=x[s, 128 * i:128 * (i + 1), :]) \
                .then_inc(*_i(cn, lds, 16))
            ld_x = N[lds]
            P.vector.wait_ge(SEM[lds], ld_x)
            P.vector.bn_stats(bs[:, 0:6], xt[b][:, 0:512])
            P.vector.bn_stats(bs[:, 6:12], xt[b][:, 512:1024])
            P.vector.drain()
            P.vector.bn_aggr(mvc, bs[:])
            P.vector.drain()
            P.vector.tensor_scalar(col("vp"), var, EPS_LN, None, ALU.add)
            P.vector.drain()
            P.vector.reciprocal(col("rvp"), col("vp"))
            P.vector.drain()
            P.vector.scalar_tensor_tensor(col("re"), var, col("rvp"), eps0_t[:],
                                          ALU.mult, ALU.add)
            P.vector.drain()
            P.vector.reciprocal(col("rre"), col("re")).then_inc(*_i(cn, "dv", 1))
            dv_pre = N["dv"]
            P.scalar.wait_ge(SEM["dv"], dv_pre)
            P.scalar.activation(col("rstd"), col("rvp"), AF.Sqrt)
            P.scalar.activation(col("r2"), col("rre"), AF.Sqrt) \
                .then_inc(*_i(cn, "ac", 1))
            ac_sqrt = N["ac"]
            P.vector.wait_ge(SEM["ac"], ac_sqrt)
            if g >= 2:
                P.vector.wait_ge(SEM["pe"], pe_done[g - 2])
                P.vector.wait_ge(SEM["ga0" if b == 0 else "ga1"], gd_acc[g - 2])
            P.vector.tensor_tensor(col("c2"), col("rstd"), col("r2"), ALU.mult)
            P.vector.tensor_scalar(col("rp1"), col("rstd"), 1.0, None, ALU.add)
            P.vector.tensor_tensor(col("mr"), mean, col("rstd"), ALU.mult)
            P.vector.drain()
            P.vector.tensor_scalar(xn_tm[b][:], xt[b][:], mean, col("rstd"),
                                   ALU.subtract, ALU.mult)
            P.vector.tensor_scalar(xnr_tm[b][:], xt[b][:], mean, col("c2"),
                                   ALU.subtract, ALU.mult)
            P.vector.tensor_scalar(acc_sb[b][:], xt[b][:], col("rp1"), col("mr"),
                                   ALU.mult, ALU.subtract) \
                .then_inc(*_i(cn, "dv", 1))
            dv_done[g] = N["dv"]
            gas = "ga0" if b == 0 else "ga1"
            P.scalar.wait_ge(SEM["dv"], dv_done[g])
            P.scalar.dma_start(out=acc0[s, 128 * i:128 * (i + 1), :],
                               in_=acc_sb[b][:]).then_inc(*_i(cn, gas, 16))
            gd_acc[g] = N[gas]
            P.tensor.wait_ge(SEM["dv"], dv_done[g])
            if g >= 1:
                P.tensor.wait_ge(SEM["ac"], ac_evac[g - 1])
            if g == 0:
                P.tensor.wait_ge(SEM["gc"], gd_const)
            for j in range(NCH):
                ins = P.tensor.transpose(pt[0][:, 128 * j:128 * (j + 1)],
                                         xn_tm[b][:, 128 * j:128 * (j + 1)], idb[:])
                if j == NCH - 1:
                    ins.then_inc(*_i(cn, "pe", 1))
            pe_xn = N["pe"]
            for j in range(NCH):
                ins = P.tensor.transpose(pt[1][:, 128 * j:128 * (j + 1)],
                                         xnr_tm[b][:, 128 * j:128 * (j + 1)], idb[:])
                if j == NCH - 1:
                    ins.then_inc(*_i(cn, "pe", 1))
            pe_done[g] = N["pe"]
            pe_xnr = N["pe"]
            xnf_v = xnf_sb.ap().rearrange("p (j t) -> p j t", j=NCH)
            xnrf_v = xnrf_sb.ap().rearrange("p (j t) -> p j t", j=NCH)
            pt0_v = pt[0].ap().rearrange("p (j t) -> p j t", j=NCH)
            pt1_v = pt[1].ap().rearrange("p (j t) -> p j t", j=NCH)
            P.scalar.wait_ge(SEM["pe"], pe_xn)
            P.scalar.activation(xnf_v[:, :, 128 * i:128 * (i + 1)], pt0_v[:, :, :],
                                AF.Copy)
            P.scalar.wait_ge(SEM["pe"], pe_xnr)
            P.scalar.activation(xnrf_v[:, :, 128 * i:128 * (i + 1)], pt1_v[:, :, :],
                                AF.Copy).then_inc(*_i(cn, "ac", 1))
            ac_evac[g] = N["ac"]
        ac_seq_done = N["ac"]
        hbm_xn = xn_fm[s].rearrange("(j p) t -> p j t", p=128)
        hbm_xnr = xnr_fm[s].rearrange("(j p) t -> p j t", p=128)
        sb_xn = xnf_sb.ap().rearrange("p (j t) -> p j t", j=NCH)
        sb_xnr = xnrf_sb.ap().rearrange("p (j t) -> p j t", j=NCH)
        P.scalar.wait_ge(SEM["ac"], ac_seq_done)
        P.scalar.dma_start(out=hbm_xn, in_=sb_xn).then_inc(*_i(cn, "gf", 16))
        P.scalar.dma_start(out=hbm_xnr, in_=sb_xnr).then_inc(*_i(cn, "gf", 16))
        gd_fm = N["gf"]
        P.tensor.wait_ge(SEM["ac"], ac_seq_done)
        for i in range(NTT):
            for j in range(NCH):
                ins = P.tensor.matmul(pg[:],
                                      xnf_sb[:, j * T + 128 * i: j * T + 128 * (i + 1)],
                                      centb[:, ER * j:ER * (j + 1)],
                                      start=(j == 0), stop=(j == NCH - 1))
                if j == NCH - 1:
                    ins.then_inc(*_i(cn, "pe", 1))
            pe_g = N["pe"]
            P.scalar.wait_ge(SEM["pe"], pe_g)
            if dv_g is not None:
                P.scalar.wait_ge(SEM["dv"], dv_g)
            P.scalar.activation(sg[:], pg[:], AF.Sigmoid).then_inc(*_i(cn, "ac", 1))
            ac_g = N["ac"]
            P.vector.wait_ge(SEM["ac"], ac_g)
            if gd_wg is not None:
                P.vector.wait_ge(SEM["gw"], gd_wg)
            P.vector.reduce_max(gsc["top1"][:], sg[:], axis=AX.X)
            P.vector.reduce_sum(gsc["den"][:], sg[:], axis=AX.X)
            P.vector.drain()
            P.vector.tensor_scalar(eqm[:], sg[:], gsc["top1"][:], None, ALU.is_ge)
            P.vector.reciprocal(gsc["rec"][:], gsc["den"][:])
            P.vector.drain()
            P.vector.scalar_tensor_tensor(smask[:], eqm[:], -1e30, sg[:],
                                          ALU.mult, ALU.add)
            P.vector.drain()
            P.vector.reduce_max(gsc["top2"][:], smask[:], axis=AX.X)
            P.vector.drain()
            P.vector.tensor_scalar(gemask[:], sg[:], gsc["top2"][:], None, ALU.is_ge)
            P.vector.tensor_scalar(wsc[:], sg[:], gsc["rec"][:], None, ALU.mult)
            P.vector.drain()
            P.vector.tensor_tensor(wgt_sb[:], wsc[:], gemask[:], ALU.mult) \
                .then_inc(*_i(cn, "dv", 1))
            dv_g = N["dv"]
            P.tensor.wait_ge(SEM["ac"], ac_g)
            P.scalar.wait_ge(SEM["dv"], dv_g)
            P.scalar.dma_start(out=wg[s, 128 * i:128 * (i + 1), :], in_=wgt_sb[:]) \
                .then_inc(*_i(cn, "gw", 16))
            gd_wg = N["gw"]
        if s + 1 < SEQ:
            pe_gates = N["pe"]
            P.scalar.wait_ge(SEM["pe"], pe_gates)
            P.scalar.wait_ge(SEM["gf"], gd_fm)

    st.close()


def emit_l0_attn(nc, P, cn, dt):
    xn_fm, acc0, acc1 = dt["xn_fm"], dt["acc0"], dt["acc1"]
    rcs, rsn = dt["rope_cs"], dt["rope_sn"]
    perm, tri = dt["perm_swap"], dt["tri128"]

    st = contextlib.ExitStack()
    sb = lambda nm, *a: st.enter_context(nc.sbuf_tensor("B_" + nm, *a))
    ps = lambda nm, *a: st.enter_context(nc.psum_tensor("B_" + nm, *a))
    xnf = sb("xnf", [128, NCH * T], BF16)
    k1b = sb("k1b", [128, 8 * T], BF16)
    k2b = sb("k2b", [128, 8 * T], BF16)
    vb = sb("vb", [128, NTT * C], BF16)
    wkq = [sb(f"wkq{n}", [128, NCH * 128], BF16) for n in range(4)]
    wvb = sb("wvb", [128, NCH * 512], BF16)
    wog = sb("wog", [128, 8 * C], BF16)
    qp = [sb(f"qp{n}", [128, 2 * 512], BF16) for n in range(2)]
    kfb = [sb(f"kfb{n}", [128, 512], BF16) for n in range(2)]
    rt = [sb(f"rt{n}", [128, 512], F32) for n in range(2)]
    e1b = [sb(f"e1b{n}", [128, 512], BF16) for n in range(2)]
    e2b = [sb(f"e2b{n}", [128, 512], BF16) for n in range(2)]
    patb = [sb(f"patb{n}", [128, 512], BF16) for n in range(4)]
    ysb = sb("ysb", [128, 8 * 512], BF16)
    acct = [sb(f"acct{n}", [128, 512], F32) for n in range(2)]
    csb = sb("csb", [128, T], BF16)
    snb = sb("snb", [128, T], BF16)
    permb = sb("permb", [128, 128], BF16)
    trib = sb("trib", [128, 128], BF16)
    fsc = sb("fsc", [128, 1], F32)
    facs = sb("facs", [128, 1], F32)
    pq = [ps(f"pq{n}", [128, 512], F32) for n in range(2)]
    ss = [ps(f"ss{n}", [128, 512], F32) for n in range(4)]
    py = ps("py", [128, 512], F32)
    po = ps("po", [128, 512], F32)

    SEM, N = cn.sem, cn.n

    P.sync.dma_start(out=csb[:], in_=rcs[:]).then_inc(*_i(cn, "lc", 16))
    P.sync.dma_start(out=snb[:], in_=rsn[:]).then_inc(*_i(cn, "lc", 16))
    P.sync.dma_start(out=permb[:], in_=perm[:]).then_inc(*_i(cn, "lc", 16))
    P.sync.dma_start(out=trib[:], in_=tri[:]).then_inc(*_i(cn, "lc", 16))
    P.sync.dma_start(out=wog.ap().rearrange("p (hp c) -> p hp c", c=C),
                     in_=dt["wouttb"].rearrange("(hp p) c -> p hp c", p=128)) \
        .then_inc(*_i(cn, "lc", 16))
    lc_all = N["lc"]
    P.vector.wait_ge(SEM["lc"], lc_all)
    P.tensor.wait_ge(SEM["lc"], lc_all)
    P.scalar.wait_ge(SEM["lc"], lc_all)
    P.vector.memset(fsc[:], 1.0)

    wku = [0]; wk_free = {}; wk_loaded = {}

    def load_wslice(srcL, idx):
        u = wku[0]; wku[0] += 1
        b = u % 4
        lws = "lw0" if b % 2 == 0 else "lw1"
        P.sync.wait_ge(SEM["pe"], wk_free.get(u - 4, 0))
        P.sync.dma_start(out=wkq[b][:], in_=srcL[idx]) \
            .then_inc(*_i(cn, lws, 16))
        wk_loaded[u] = (lws, N[lws])
        return u, b

    pqu = [0]; pq_free = {}
    ssu = [0]; ss_free = {}
    patu = [0]; pat_free = {}
    e1u = [0]; e1_free = {}
    rope_done = {}
    kf_evac = {}
    pend = [None]

    def chunk_proj(wb, nslc, kdst_ap, tslc, wwait, kdst_pe=None):
        u = pqu[0]; pqu[0] += 1
        pb = pq[u % 2]
        P.tensor.wait_ge(SEM[wwait[0]], wwait[1])
        if (u - 2) in pq_free:
            P.tensor.wait_ge(SEM["ac"], pq_free[u - 2])
        for cj in range(NCH):
            ins = P.tensor.matmul(
                pb[:], wb[:, cj * 128:(cj + 1) * 128],
                xnf[:, cj * T + nslc.start: cj * T + nslc.stop],
                start=(cj == 0), stop=(cj == NCH - 1))
        ins.then_inc(*_i(cn, "pe", 1))
        pe_proj = N["pe"]
        P.scalar.wait_ge(SEM["pe"], pe_proj)
        if (u - 2) in rope_done:
            P.scalar.wait_ge(SEM["dv"], rope_done[u - 2])
        P.scalar.activation(kfb[u % 2][:], pb[:], AF.Copy) \
            .then_inc(*_i(cn, "ac", 1))
        pq_free[u] = N["ac"]
        P.scalar.activation(facs[:], fsc[:], AF.Copy) \
            .then_inc(*_i(cn, "ac", 1))
        kf_evac[u] = N["ac"]
        if pend[0] is not None:
            swap_rope(*pend[0])
        pend[0] = (u, kdst_ap, tslc, kdst_pe)
        return u

    def swap_rope(u, kdst_ap, tslc, kdst_pe):
        su = ssu[0]; ssu[0] += 1
        psw = ss[su % 4]
        if (su - 4) in ss_free:
            d_, a_ = ss_free[su - 4]
            P.tensor.wait_ge(SEM["dv"], d_)
            P.tensor.wait_ge(SEM["ac"], a_)
        P.tensor.wait_ge(SEM["ac"], kf_evac[u])
        P.tensor.matmul(psw[:], permb[:], kfb[u % 2][:],
                        start=True, stop=True).then_inc(*_i(cn, "pe", 1))
        pe_sw = N["pe"]
        P.vector.wait_ge(SEM["pe"], pe_sw)
        P.vector.drain()
        P.vector.tensor_tensor(rt[0][:], kfb[u % 2][:], csb[:, tslc], ALU.mult)
        P.vector.tensor_tensor(rt[1][:], psw[:], snb[:, tslc], ALU.mult)
        P.vector.drain()
        if kdst_pe is not None:
            P.vector.wait_ge(SEM["pe"], kdst_pe)
        P.vector.tensor_tensor(kdst_ap, rt[0][:], rt[1][:], ALU.add) \
            .then_inc(*_i(cn, "dv", 1))
        rope_done[u] = N["dv"]
        ss_free[su] = (N["dv"], N["ac"])

    def flush_pend():
        if pend[0] is not None:
            swap_rope(*pend[0])
            pend[0] = None

    pe_seq_end = 0
    pcc = [0]
    pe_smm_qp = {}
    dv_q_ready = {}
    ac_ysb = {}
    dv_acct = {}
    sa_acct = {}
    pe_oproj_end = [0]
    for s in range(SEQ):
        if s > 0:
            P.sync.wait_ge(SEM["pe"], pe_seq_end)
        pre0 = load_wslice(dt["wk1L"], 0)
        pre1_ = load_wslice(dt["wk2L"], 0)
        P.sync.dma_start(out=xnf.ap().rearrange("p (j t) -> p j t", j=NCH),
                         in_=xn_fm[s].rearrange("(j p) t -> p j t", p=128)) \
            .then_inc(*_i(cn, "lx", 16))
        lx_t = N["lx"]
        P.sync.dma_start(out=trib[:], in_=tri[:]).then_inc(*_i(cn, "lz", 16))
        lz_t = N["lz"]
        first = [True]
        for hp in range(8):
            for wL, kdst in ((dt["wk1L"], k1b), (dt["wk2L"], k2b)):
                if hp == 0:
                    u_w, b = pre0 if kdst is k1b else pre1_
                else:
                    u_w, b = load_wslice(wL, hp)
                for nt in range(4):
                    ts_ = slice(nt * 512, (nt + 1) * 512)
                    if first[0]:
                        P.tensor.wait_ge(SEM["lz"], lz_t)
                        P.tensor.wait_ge(SEM["lx"], lx_t)
                        first[0] = False
                    chunk_proj(wkq[b], ts_,
                               k1b[:, hp * T + nt * 512:
                                   hp * T + (nt + 1) * 512] if kdst is k1b
                               else k2b[:, hp * T + nt * 512:
                                        hp * T + (nt + 1) * 512],
                               ts_, wk_loaded[u_w])
                wk_free[u_w] = N["pe"]
        for ha in range(2):
            if ha == 1:
                P.sync.wait_ge(SEM["pe"], N["pe"])
            P.sync.dma_start(out=wvb[:], in_=dt["wvL"][ha]) \
                .then_inc(*_i(cn, "lv0", 16))
            lv_t = N["lv0"]
            P.tensor.wait_ge(SEM["lv0"], lv_t)
            for i in range(NTT):
                su = ssu[0]; ssu[0] += 1
                pv = ss[su % 4]
                if (su - 4) in ss_free:
                    d_, a_ = ss_free[su - 4]
                    P.tensor.wait_ge(SEM["dv"], d_)
                    P.tensor.wait_ge(SEM["ac"], a_)
                for cj in range(NCH):
                    ins = P.tensor.matmul(
                        pv[:], xnf[:, cj * T + i * 128: cj * T + (i + 1) * 128],
                        wvb[:, cj * 512:(cj + 1) * 512],
                        start=(cj == 0), stop=(cj == NCH - 1))
                ins.then_inc(*_i(cn, "pe", 1))
                P.scalar.wait_ge(SEM["pe"], N["pe"])
                P.scalar.activation(
                    vb[:, i * C + ha * 512: i * C + (ha + 1) * 512],
                    pv[:], AF.Copy, scale=INV_D2).then_inc(*_i(cn, "ac", 1))
                ss_free[su] = (N["dv"], N["ac"])
        P.scalar.activation(facs[:], fsc[:], AF.Copy).then_inc(*_i(cn, "ac", 1))
        ac_v_done = N["ac"]
        flush_pend()
        P.vector.tensor_scalar(fsc[:], fsc[:], 1.0, None, ALU.mult) \
            .then_inc(*_i(cn, "dv", 1))
        dv_k_done = N["dv"]
        first_smm = [True]

        def build_q(qt, hp):
            pc = pcc[0]; pcc[0] += 1
            qcols = slice(qt * 512 + 0, qt * 512 + 512)
            for wi, wL in ((0, dt["wq1L"]), (1, dt["wq2L"])):
                u_w, b = load_wslice(wL, hp)
                chunk_proj(wkq[b], qcols,
                           qp[pc % 2][:, wi * 512:(wi + 1) * 512],
                           qcols, wk_loaded[u_w],
                           kdst_pe=pe_smm_qp.get(pc - 2, 0))
                wk_free[u_w] = N["pe"]
            return pc

        qpars = {(0, 0): build_q(0, 0)}
        for qt in range(4):
            nkt = 4 * qt + 4
            for hp in range(8):
                nxt = (qt, hp + 1) if hp < 7 else \
                    ((qt + 1, 0) if qt < 3 else None)
                if nxt is not None:
                    qpars[nxt] = build_q(*nxt)
                else:
                    flush_pend()
                P.vector.tensor_scalar(fsc[:], fsc[:], 1.0, None, ALU.mult) \
                    .then_inc(*_i(cn, "dv", 1))
                dv_q_ready[(qt, hp)] = N["dv"]
                pc = qpars[(qt, hp)]
                qpar = pc % 2
                for hh in range(2):
                    h = 2 * hp + hh
                    info = {}
                    pes = {}
                    ac_e1 = {}
                    dv_pat_after = {}

                    def emit_smm(kt):
                        su = ssu[0]; ssu[0] += 2
                        s1p, s2p = ss[su % 4], ss[(su + 1) % 4]
                        for du in (su - 4, su - 3):
                            if du in ss_free:
                                d_, a_ = ss_free[du]
                                P.tensor.wait_ge(SEM["dv"], d_)
                                P.tensor.wait_ge(SEM["ac"], a_)
                        if first_smm[0]:
                            P.tensor.wait_ge(SEM["dv"], dv_k_done)
                            first_smm[0] = False
                        if kt == 0 and hh == 0:
                            P.tensor.wait_ge(SEM["dv"], dv_q_ready[(qt, hp)])
                        ins = P.tensor.matmul(
                            s1p[:],
                            k1b[64 * hh:64 * hh + 64,
                                hp * T + kt * 128: hp * T + (kt + 1) * 128],
                            qp[qpar][64 * hh:64 * hh + 64, 0:512],
                            start=True, stop=True)
                        ins.then_inc(*_i(cn, "pe", 1))
                        ins = P.tensor.matmul(
                            s2p[:],
                            k2b[64 * hh:64 * hh + 64,
                                hp * T + kt * 128: hp * T + (kt + 1) * 128],
                            qp[qpar][64 * hh:64 * hh + 64, 512:1024],
                            start=True, stop=True)
                        ins.then_inc(*_i(cn, "pe", 1))
                        pes[kt] = (su, N["pe"])

                    def emit_e1(kt):
                        eu = e1u[0]; e1u[0] += 1
                        su, pe_s = pes[kt]
                        P.scalar.wait_ge(SEM["pe"], pe_s)
                        if (eu - 2) in e1_free:
                            P.scalar.wait_ge(SEM["dv"], e1_free[eu - 2])
                        P.scalar.activation(e1b[eu % 2][:], ss[su % 4][:],
                                            AF.Copy) \
                            .then_inc(*_i(cn, "ac", 1))
                        P.vector.wait_ge(SEM["pe"], pe_s)
                        if (eu - 2) in e1_free:
                            P.vector.wait_ge(SEM["pe"],
                                             pat_free.get(eu - 2, 0))
                        P.vector.tensor_copy(e2b[eu % 2][:],
                                             ss[(su + 1) % 4][:]) \
                            .then_inc(*_i(cn, "dv", 1))
                        ac_e1[kt] = (eu, N["ac"], N["dv"])
                        ss_free[su] = (N["dv"], N["ac"])
                        ss_free[su + 1] = (N["dv"], N["ac"])

                    def emit_patm(kt, fence):
                        pu = patu[0]; patu[0] += 1
                        eu = ac_e1[kt][0]
                        r = pu % 4
                        P.vector.wait_ge(SEM["ac"], fence[0])
                        if (pu - 4) in pat_free:
                            P.vector.wait_ge(SEM["pe"], pat_free[pu - 4])
                        lo = 0
                        if kt >= 4 * qt:
                            j = kt - 4 * qt
                            lo = 128 * j
                            P.vector.drain()
                            P.vector.tensor_tensor(
                                e1b[eu % 2][:, lo:lo + 128],
                                e1b[eu % 2][:, lo:lo + 128], trib[:], ALU.mult)
                            P.vector.drain()
                        P.vector.tensor_tensor(patb[r][:, lo:512],
                                               e2b[eu % 2][:, lo:512],
                                               e1b[eu % 2][:, lo:512],
                                               ALU.mult) \
                            .then_inc(*_i(cn, "dv", 1))
                        e1_free[eu] = N["dv"]
                        dv_pat_after[kt] = N["dv"]
                        info[kt] = (r, lo, pu)

                    def emit_ymm(kt, dv_need):
                        r, lo, pu = info[kt]
                        P.tensor.wait_ge(SEM["dv"], dv_need)
                        if kt == 0 and hh == 0:
                            P.tensor.wait_ge(SEM["ac"], ac_v_done)
                            P.tensor.wait_ge(SEM["ac"], ac_ysb.get(hp - 1, 0))
                            if hp == 0:
                                P.tensor.wait_ge(SEM["dv"],
                                                 dv_acct.get(1, 0))
                        ins = P.tensor.matmul(
                            py[64 * hh:64 * hh + 64, lo:512],
                            vb[:, kt * C + h * 64: kt * C + h * 64 + 64],
                            patb[r][:, lo:512],
                            start=(kt == 0), stop=(kt == nkt - 1))
                        ins.then_inc(*_i(cn, "pe", 1))
                        pat_free[pu] = N["pe"]

                    for kt in range(nkt):
                        emit_smm(kt)
                        emit_e1(kt)
                        if kt >= 1:
                            emit_patm(kt - 1, (ac_e1[kt][1], ac_e1[kt][2]))
                        if kt >= 3:
                            emit_ymm(kt - 3, dv_pat_after[kt - 2])
                    P.scalar.activation(facs[:], fsc[:], AF.Copy) \
                        .then_inc(*_i(cn, "ac", 1))
                    emit_patm(nkt - 1, (N["ac"], N["dv"]))
                    P.vector.tensor_scalar(fsc[:], fsc[:], 1.0, None,
                                           ALU.mult).then_inc(*_i(cn, "dv", 1))
                    dv_tail = N["dv"]
                    if nkt >= 3:
                        emit_ymm(nkt - 3, dv_pat_after[nkt - 2])
                    emit_ymm(nkt - 2, dv_tail)
                    emit_ymm(nkt - 1, dv_tail)
                pe_smm_qp[pc] = N["pe"]
                P.scalar.wait_ge(SEM["pe"], N["pe"])
                if hp == 0:
                    P.scalar.wait_ge(SEM["pe"], pe_oproj_end[0])
                P.scalar.activation(ysb[:, hp * 512:(hp + 1) * 512], py[:],
                                    AF.Copy)
                P.scalar.activation(facs[:], fsc[:], AF.Copy) \
                    .then_inc(*_i(cn, "ac", 1))
                ac_ysb[hp] = N["ac"]
            P.tensor.wait_ge(SEM["ac"], ac_ysb[7])
            for t8 in range(8):
                sl, ha = t8 // 2, t8 % 2
                bpar = t8 % 2
                row0 = qt * 512 + sl * 128
                csl = slice(ha * 512, (ha + 1) * 512)
                las = "la0" if bpar == 0 else "la1"
                sas = "sa0" if bpar == 0 else "sa1"
                if bpar in sa_acct:
                    P.sync.wait_ge(SEM[sas], sa_acct[bpar])
                P.sync.dma_start(out=acct[bpar][:],
                                 in_=acc0[s, row0:row0 + 128, csl]) \
                    .then_inc(*_i(cn, las, 16))
                la_t = N[las]
                if bpar in dv_acct:
                    P.tensor.wait_ge(SEM["dv"], dv_acct[bpar])
                pob = po if bpar == 0 else py
                for hp in range(8):
                    ins = P.tensor.matmul(
                        pob[:],
                        ysb[:, hp * 512 + sl * 128: hp * 512 + sl * 128 + 128],
                        wog[:, hp * C + ha * 512: hp * C + (ha + 1) * 512],
                        start=(hp == 0), stop=(hp == 7))
                ins.then_inc(*_i(cn, "pe", 1))
                pe_o = N["pe"]
                P.vector.wait_ge(SEM["pe"], pe_o)
                P.vector.wait_ge(SEM[las], la_t)
                P.vector.drain()
                P.vector.tensor_tensor(acct[bpar][:], acct[bpar][:], pob[:],
                                       ALU.add)
                P.vector.tensor_scalar(fsc[:], fsc[:], 1.0, None, ALU.mult) \
                    .then_inc(*_i(cn, "dv", 1))
                dv_acct[bpar] = N["dv"]
                P.scalar.wait_ge(SEM["dv"], dv_acct[bpar])
                P.scalar.dma_start(out=acc1[s, row0:row0 + 128, csl],
                                   in_=acct[bpar][:]) \
                    .then_inc(*_i(cn, sas, 16))
                sa_acct[bpar] = N[sas]
            pe_oproj_end[0] = N["pe"]
        pe_seq_end = N["pe"]

    st.close()


def emit_l0_ffn(nc, P, cn, dt):
    xnr_fm, acc1, wg, x1 = dt["xnr_fm"], dt["acc1"], dt["wg"], dt["x1"]
    b1s, b1r = dt["b1s"], dt["b1r"]
    g_s, g_r = dt["g_s"], dt["g_r"]

    st = contextlib.ExitStack()
    sb = lambda nm, *a: st.enter_context(nc.sbuf_tensor("C_" + nm, *a))
    ps = lambda nm, *a: st.enter_context(nc.psum_tensor("C_" + nm, *a))
    xb = [sb(f"xb{n}", [128, NCH * 512], BF16) for n in range(2)]
    xgs = [sb(f"xgs{n}", [128, NCH * 512], BF16) for n in range(2)]
    xgr = [sb(f"xgr{n}", [128, NCH * 512], BF16) for n in range(2)]
    accs = [sb(f"accs{n}", [128, 4 * C], F32) for n in range(2)]
    wgt4 = [sb(f"wgt4{n}", [128, 4 * ER], F32) for n in range(2)]
    hfm = [sb(f"hfm{n}", [128, NHCH * 512], BF16) for n in range(2)]
    w1t = [sb(f"w1t{n}", [128, NCH * 512], BF16) for n in range(2)]
    w2t = [sb(f"w2t{n}", [128, 4 * C], BF16) for n in range(2)]
    b1sb = sb("b1sb", [128, ES * 32], F32)
    b1rb = sb("b1rb", [128, ER * 32], F32)
    b2sb = sb("b2sb", [1, ES * C], BF16)
    b2rb = sb("b2rb", [1, ER * C], BF16)
    gsb = sb("gsb", [128, NCH], F32)
    grb = sb("grb", [128, NCH], F32)
    onesb = sb("onesb", [1, 128], BF16)
    pout = [ps(f"pout{n}", [128, 512], F32) for n in range(8)]

    SEM, N = cn.sem, cn.n

    P.sync.dma_start(out=b1sb[:], in_=b1s[:]).then_inc(*_i(cn, "lb", 16))
    P.sync.dma_start(out=b1rb[:], in_=b1r[:]).then_inc(*_i(cn, "lb", 16))
    P.sync.dma_start(out=gsb[:], in_=g_s[:]).then_inc(*_i(cn, "lb", 16))
    P.sync.dma_start(out=grb[:], in_=g_r[:]).then_inc(*_i(cn, "lb", 16))
    P.sync.dma_start(out=b2sb[:], in_=dt["b2sbb"][:]).then_inc(*_i(cn, "lb", 16))
    P.sync.dma_start(out=b2rb[:], in_=dt["b2rbb"][:]).then_inc(*_i(cn, "lb", 16))
    P.sync.dma_start(out=onesb[:], in_=dt["d_onesb"][:]).then_inc(*_i(cn, "lb", 16))
    lb_all = N["lb"]
    P.vector.wait_ge(SEM["lb"], lb_all)
    P.scalar.wait_ge(SEM["lb"], lb_all)
    P.tensor.wait_ge(SEM["lb"], lb_all)
    P.sync.wait_ge(SEM["pre1"], cn.n["pre1"])

    w1u = [0]; w1_free = {-1: 0, -2: 0}
    w2u = [0]; w2_free = {-1: 0, -2: 0}
    lx_tb, la_tb, lg_tb, dv_xg = {}, {}, {}, {}
    sa_tb = {-1: 0, -2: 0}
    pe_tb_end = {-1: 0, -2: 0}
    dv_acc_end = {-1: 0, -2: 0}
    dv_bank = {b: 0 for b in range(8)}
    pe_out_exp = {}
    ecnt = [0]

    NTB = SEQ * 4
    GROUPS = ((ES, (dt["w1Ls"], dt["w2Ls"]), b1sb, b2sb, xgs),
              (ER, (dt["w1Lr"], dt["w2Lr"]), b1rb, b2rb, xgr))

    def emit_tb_loads(k):
        s, tb = k // 4, k % 4
        par = k % 2
        P.sync.wait_ge(SEM["dv"], dv_xg.get(k - 2, 0))
        hb3 = xnr_fm[s].rearrange("(j p) t -> p j t", p=128)[
            :, :, tb * 512:(tb + 1) * 512]
        xb3 = xb[par].ap().rearrange("p (j t) -> p j t", j=NCH)
        P.sync.dma_start(out=xb3, in_=hb3).then_inc(*_i(cn, "lx", 16))
        lx_tb[k] = N["lx"]
        P.sync.wait_ge(SEM["sa"], sa_tb[k - 2])
        ac3 = acc1[s, tb * 512:(tb + 1) * 512, :] \
            .rearrange("(tt p) c -> p tt c", p=128)
        as3 = accs[par].ap().rearrange("p (tt c) -> p tt c", tt=4)
        P.sync.dma_start(out=as3, in_=ac3).then_inc(*_i(cn, "la", 16))
        la_tb[k] = N["la"]
        P.sync.wait_ge(SEM["dv"], dv_acc_end[k - 2])
        wg3 = wg[s, tb * 512:(tb + 1) * 512, :] \
            .rearrange("(tt p) e -> p tt e", p=128)
        wg4 = wgt4[par].ap().rearrange("p (tt e) -> p tt e", tt=4)
        P.sync.dma_start(out=wg4, in_=wg3).then_inc(*_i(cn, "lg", 16))
        lg_tb[k] = N["lg"]

    def emit_xg(k):
        par = k % 2
        P.vector.wait_ge(SEM["lx"], lx_tb[k])
        P.vector.wait_ge(SEM["pe"], pe_tb_end[k - 2])
        for gt, dst in ((gsb, xgs[par]), (grb, xgr[par])):
            for j in range(NCH):
                ins = P.vector.tensor_scalar(
                    dst[:, j * 512:(j + 1) * 512],
                    xb[par][:, j * 512:(j + 1) * 512],
                    gt[:, j:j + 1], None, ALU.mult)
        ins.then_inc(*_i(cn, "dv", 1))
        dv_xg[k] = N["dv"]

    emit_tb_loads(0)
    emit_xg(0)
    for k in range(NTB):
        s, tb = k // 4, k % 4
        par = k % 2
        for grp, (ne, (w1L, w2L), b1t, b2t, xgb) in enumerate(GROUPS):
            xg = xgb[par]
            for e in range(ne):
                ec = ecnt[0]; ecnt[0] += 1
                epar = ec % 2
                ac_gelu = {}
                for hb in range(8):
                    u = w1u[0]; w1u[0] += 1
                    parw = u % 2
                    P.sync.wait_ge(SEM["pe"], w1_free[u - 2])
                    P.sync.dma_start(out=w1t[parw][:], in_=w1L[e, hb]) \
                        .then_inc(*_i(cn, "lw0", 16))
                    lw_t = N["lw0"]
                    P.tensor.wait_ge(SEM["lw0"], lw_t)
                    for hc in range(4):
                        hg = hb * 4 + hc
                        pb = pout[6 + (hg & 1)]
                        if hg < 2:
                            P.tensor.wait_ge(SEM["dv"], dv_bank[6 + hg])
                            if hg == 0:
                                P.tensor.wait_ge(SEM["dv"], dv_xg[k])
                        else:
                            P.tensor.wait_ge(SEM["ac"], ac_gelu[hg - 2])
                        for cj in range(NCH):
                            ins = P.tensor.matmul(
                                pb[:],
                                w1t[parw][:, cj * 512 + hc * 128:
                                          cj * 512 + (hc + 1) * 128],
                                xg[:, cj * 512:(cj + 1) * 512],
                                start=(cj == 0), stop=(cj == NCH - 1))
                        ins.then_inc(*_i(cn, "pe", 1))
                        pe_h = N["pe"]
                        P.scalar.wait_ge(SEM["pe"], pe_h)
                        if hg == 0:
                            P.scalar.wait_ge(SEM["pe"],
                                             pe_out_exp.get(ec - 2, 0))
                        P.scalar.activation(
                            hfm[epar][:, hg * 512:(hg + 1) * 512], pb[:],
                            AF.Gelu,
                            bias=b1t[:, e * 32 + hg:e * 32 + hg + 1]) \
                            .then_inc(*_i(cn, "ac", 1))
                        ac_gelu[hg] = N["ac"]
                    w1_free[u] = N["pe"]
                ac_h_done = N["ac"]
                P.tensor.wait_ge(SEM["ac"], ac_h_done)
                for b8 in range(8):
                    tt, ch = b8 // 2, b8 % 2
                    P.tensor.wait_ge(SEM["dv"], dv_bank[b8])
                    ins = P.tensor.matmul(
                        pout[b8][:], onesb[:],
                        b2t[:, e * C + ch * 512:e * C + (ch + 1) * 512],
                        start=True, stop=False)
                for g32 in range(8):
                    u = w2u[0]; w2u[0] += 1
                    parv = u % 2
                    P.sync.wait_ge(SEM["pe"], w2_free[u - 2])
                    P.sync.dma_start(out=w2t[parv][:], in_=w2L[e, g32]) \
                        .then_inc(*_i(cn, "lv0", 16))
                    lv_t = N["lv0"]
                    P.tensor.wait_ge(SEM["lv0"], lv_t)
                    for q in range(4):
                        hc32 = g32 * 4 + q
                        for b8 in range(8):
                            tt, ch = b8 // 2, b8 % 2
                            ins = P.tensor.matmul(
                                pout[b8][:],
                                hfm[epar][:, hc32 * 512 + tt * 128:
                                          hc32 * 512 + (tt + 1) * 128],
                                w2t[parv][:, q * C + ch * 512:
                                          q * C + (ch + 1) * 512],
                                start=False, stop=(hc32 == NHCH - 1))
                        ins.then_inc(*_i(cn, "pe", 1))
                    w2_free[u] = N["pe"]
                pe_out = N["pe"]
                pe_out_exp[ec] = pe_out
                P.vector.wait_ge(SEM["pe"], pe_out)
                P.vector.wait_ge(SEM["la"], la_tb[k])
                P.vector.wait_ge(SEM["lg"], lg_tb[k])
                P.vector.drain()
                for b8 in (6, 7, 0, 1, 2, 3, 4, 5):
                    tt, ch = b8 // 2, b8 % 2
                    asl = accs[par][:, tt * C + ch * 512:
                                    tt * C + (ch + 1) * 512]
                    if grp == 0:
                        ins = P.vector.tensor_tensor(asl, asl, pout[b8][:],
                                                     ALU.add)
                    else:
                        ins = P.vector.scalar_tensor_tensor(
                            asl, pout[b8][:],
                            wgt4[par][:, tt * ER + e: tt * ER + e + 1],
                            asl, ALU.mult, ALU.add)
                    ins.then_inc(*_i(cn, "dv", 1))
                    dv_bank[b8] = N["dv"]
                dv_acc_end[k] = N["dv"]
            if grp == 0 and k + 1 < NTB:
                emit_tb_loads(k + 1)
                emit_xg(k + 1)
        P.scalar.wait_ge(SEM["dv"], dv_acc_end[k])
        x13 = x1[s, tb * 512:(tb + 1) * 512, :] \
            .rearrange("(tt p) c -> p tt c", p=128)
        as3 = accs[par].ap().rearrange("p (tt c) -> p tt c", tt=4)
        P.scalar.dma_start(out=x13, in_=as3).then_inc(*_i(cn, "sa", 16))
        sa_tb[k] = N["sa"]
        pe_tb_end[k] = N["pe"]

    st.close()


def emit_l1_prep(nc, P, cn, dt):
    x = dt["x1"]
    na_fm, nf_fm = dt["na_fm"], dt["nf_fm"]

    st = contextlib.ExitStack()
    sb = lambda nm, *a: st.enter_context(nc.sbuf_tensor("D_" + nm, *a))
    ps = lambda nm, *a: st.enter_context(nc.psum_tensor("D_" + nm, *a))
    xt = [sb(f"xt{n}", [128, C], F32) for n in range(2)]
    na_tm = [sb(f"natm{n}", [128, C], BF16) for n in range(2)]
    nf_tm = [sb(f"nftm{n}", [128, C], BF16) for n in range(2)]
    bs = sb("bs", [128, 12], F32)
    mv = sb("mv", [128, 2 * NTT], F32)
    stats = {nm: sb("st_" + nm, [128, NTT], F32)
             for nm in ("vp", "rvp", "re", "rre", "rstd", "r2", "c2")}
    eps1_t = sb("eps1t", [128, 1], F32)
    idb = sb("idb", [128, 128], BF16)
    naf_sb = sb("naf", [128, NCH * T], BF16)
    nff_sb = sb("nff", [128, NCH * T], BF16)
    pt = [ps(f"pt{n}", [128, 1024], BF16) for n in range(2)]

    SEM, N = cn.sem, cn.n
    P.sync.dma_start(out=idb[:], in_=dt["d_identb"][:]).then_inc(*_i(cn, "gc", 16))
    gd_const = N["gc"]
    P.vector.memset(eps1_t[:], EPS_RMS1)

    dv_done = {}
    pe_done = {}
    ac_evac = {}
    for s in range(SEQ):
        for i in range(NTT):
            g = s * NTT + i
            b = g % 2
            mean = mv[:, 2 * i:2 * i + 1]
            var = mv[:, 2 * i + 1:2 * i + 2]
            col = lambda nm: stats[nm][:, i:i + 1]
            lds = "ld0" if b == 0 else "ld1"
            if g >= 2:
                P.sync.wait_ge(SEM["dv"], dv_done[g - 2])
            P.sync.dma_start(out=xt[b][:], in_=x[s, 128 * i:128 * (i + 1), :]) \
                .then_inc(*_i(cn, lds, 16))
            ld_x = N[lds]
            P.vector.wait_ge(SEM[lds], ld_x)
            P.vector.bn_stats(bs[:, 0:6], xt[b][:, 0:512])
            P.vector.bn_stats(bs[:, 6:12], xt[b][:, 512:1024])
            P.vector.drain()
            P.vector.bn_aggr(mv[:, 2 * i:2 * i + 2], bs[:])
            P.vector.drain()
            P.vector.tensor_scalar(col("vp"), var, EPS_LN, None, ALU.add)
            P.vector.drain()
            P.vector.reciprocal(col("rvp"), col("vp"))
            P.vector.drain()
            P.vector.scalar_tensor_tensor(col("re"), var, col("rvp"), eps1_t[:],
                                          ALU.mult, ALU.add)
            P.vector.drain()
            P.vector.reciprocal(col("rre"), col("re")).then_inc(*_i(cn, "dv", 1))
            dv_pre = N["dv"]
            P.scalar.wait_ge(SEM["dv"], dv_pre)
            P.scalar.activation(col("rstd"), col("rvp"), AF.Sqrt)
            P.scalar.activation(col("r2"), col("rre"), AF.Sqrt) \
                .then_inc(*_i(cn, "ac", 1))
            ac_sqrt = N["ac"]
            P.vector.wait_ge(SEM["ac"], ac_sqrt)
            if g >= 2:
                P.vector.wait_ge(SEM["pe"], pe_done[g - 2])
            P.vector.tensor_tensor(col("c2"), col("rstd"), col("r2"), ALU.mult)
            P.vector.drain()
            P.vector.tensor_scalar(nf_tm[b][:], xt[b][:], mean, col("rstd"),
                                   ALU.subtract, ALU.mult)
            P.vector.tensor_scalar(na_tm[b][:], xt[b][:], mean, col("c2"),
                                   ALU.subtract, ALU.mult) \
                .then_inc(*_i(cn, "dv", 1))
            dv_done[g] = N["dv"]
            P.tensor.wait_ge(SEM["dv"], dv_done[g])
            if g >= 1:
                P.tensor.wait_ge(SEM["ac"], ac_evac[g - 1])
            if g == 0:
                P.tensor.wait_ge(SEM["gc"], gd_const)
            for j in range(NCH):
                ins = P.tensor.transpose(pt[0][:, 128 * j:128 * (j + 1)],
                                         na_tm[b][:, 128 * j:128 * (j + 1)], idb[:])
                if j == NCH - 1:
                    ins.then_inc(*_i(cn, "pe", 1))
            pe_na = N["pe"]
            for j in range(NCH):
                ins = P.tensor.transpose(pt[1][:, 128 * j:128 * (j + 1)],
                                         nf_tm[b][:, 128 * j:128 * (j + 1)], idb[:])
                if j == NCH - 1:
                    ins.then_inc(*_i(cn, "pe", 1))
            pe_done[g] = N["pe"]
            naf_v = naf_sb.ap().rearrange("p (j t) -> p j t", j=NCH)
            nff_v = nff_sb.ap().rearrange("p (j t) -> p j t", j=NCH)
            pt0_v = pt[0].ap().rearrange("p (j t) -> p j t", j=NCH)
            pt1_v = pt[1].ap().rearrange("p (j t) -> p j t", j=NCH)
            P.scalar.wait_ge(SEM["pe"], pe_na)
            P.scalar.activation(naf_v[:, :, 128 * i:128 * (i + 1)], pt0_v[:, :, :],
                                AF.Copy)
            P.scalar.wait_ge(SEM["pe"], pe_done[g])
            P.scalar.activation(nff_v[:, :, 128 * i:128 * (i + 1)], pt1_v[:, :, :],
                                AF.Copy).then_inc(*_i(cn, "ac", 1))
            ac_evac[g] = N["ac"]
        ac_seq = N["ac"]
        hbm_na = na_fm[s].rearrange("(j p) t -> p j t", p=128)
        hbm_nf = nf_fm[s].rearrange("(j p) t -> p j t", p=128)
        sb_na = naf_sb.ap().rearrange("p (j t) -> p j t", j=NCH)
        sb_nf = nff_sb.ap().rearrange("p (j t) -> p j t", j=NCH)
        P.scalar.wait_ge(SEM["ac"], ac_seq)
        P.scalar.dma_start(out=hbm_na, in_=sb_na).then_inc(*_i(cn, "gf", 16))
        P.scalar.dma_start(out=hbm_nf, in_=sb_nf).then_inc(*_i(cn, "gf", 16))
        gd_fm = N["gf"]
        if s + 1 < SEQ:
            P.scalar.wait_ge(SEM["gf"], gd_fm)

    st.close()


def emit_l1_scan(nc, P, cn, dt):
    na_fm, hx_fm = dt["na_fm"], dt["hx_fm"]
    wgt, wct, lnw = dt["wgtb"], dt["wctb"], dt["ln_w"]

    st = contextlib.ExitStack()
    sb = lambda nm, *a: st.enter_context(nc.sbuf_tensor("E_" + nm, *a))
    ps = lambda nm, *a: st.enter_context(nc.psum_tensor("E_" + nm, *a))
    naf = sb("naf", [128, NCH * T], BF16)
    wgb = sb("wgb", [128, NCH * C], BF16)
    wcb = sb("wcb", [128, NCH * C], BF16)
    lnwb = sb("lnwb", [128, NCH], F32)
    a_sb = sb("a_sb", [128, T], F32)
    xsc = sb("xsc", [128, T], F32)
    beta = [sb(f"beta{n}", [128, 512], F32) for n in range(2)]
    hxc = sb("hxc", [128, T], F32)
    pb = [ps(f"pb{n}", [128, 512], F32) for n in range(4)]

    SEM, N = cn.sem, cn.n
    P.sync.dma_start(out=wgb.ap().rearrange("p (j c) -> p j c", j=NCH),
                     in_=wgt.rearrange("(j p) c -> p j c", p=128)) \
        .then_inc(*_i(cn, "lw", 16))
    P.sync.dma_start(out=wcb.ap().rearrange("p (j c) -> p j c", j=NCH),
                     in_=wct.rearrange("(j p) c -> p j c", p=128)) \
        .then_inc(*_i(cn, "lw", 16))
    lw_all = N["lw"]
    P.sync.dma_start(out=lnwb[:], in_=lnw[:]).then_inc(*_i(cn, "lg", 16))
    lg_all = N["lg"]
    P.vector.wait_ge(SEM["lw"], lw_all)
    P.vector.wait_ge(SEM["lg"], lg_all)
    for dj in range(NCH):
        P.vector.tensor_scalar(wgb[:, dj * C:(dj + 1) * C],
                               wgb[:, dj * C:(dj + 1) * C],
                               lnwb[:, dj:dj + 1], None, ALU.mult)
        ins = P.vector.tensor_scalar(wcb[:, dj * C:(dj + 1) * C],
                                     wcb[:, dj * C:(dj + 1) * C],
                                     lnwb[:, dj:dj + 1], None, ALU.mult)
    ins.then_inc(*_i(cn, "dv", 1))
    dv_fold = N["dv"]
    P.tensor.wait_ge(SEM["dv"], dv_fold)

    dv_scan_read = 0
    sh_prev = 0
    nu_c = [0]
    dv_xsc = {-1: 0, -2: 0}
    for s in range(SEQ):
        if s > 0:
            P.sync.wait_ge(SEM["pe"], pe_seq_last)
        P.sync.dma_start(out=naf.ap().rearrange("p (j t) -> p j t", j=NCH),
                         in_=na_fm[s].rearrange("(j p) t -> p j t", p=128)) \
            .then_inc(*_i(cn, "lx", 16))
        lx_t = N["lx"]
        P.tensor.wait_ge(SEM["lx"], lx_t)
        for j in range(NCH):
            for nt in range(4):
                nu = nu_c[0]; nu_c[0] += 1
                p1, p2 = pb[(nu % 2) * 2], pb[(nu % 2) * 2 + 1]
                P.tensor.wait_ge(SEM["dv"], dv_xsc[nu - 2])
                for dj in range(NCH):
                    ins = P.tensor.matmul(
                        p1[:], wgb[:, dj * C + j * 128: dj * C + (j + 1) * 128],
                        naf[:, dj * T + nt * 512: dj * T + (nt + 1) * 512],
                        start=(dj == 0), stop=(dj == NCH - 1))
                ins.then_inc(*_i(cn, "pe", 1))
                for dj in range(NCH):
                    ins = P.tensor.matmul(
                        p2[:], wcb[:, dj * C + j * 128: dj * C + (j + 1) * 128],
                        naf[:, dj * T + nt * 512: dj * T + (nt + 1) * 512],
                        start=(dj == 0), stop=(dj == NCH - 1))
                ins.then_inc(*_i(cn, "pe", 1))
                pe_p = N["pe"]
                P.scalar.wait_ge(SEM["pe"], pe_p)
                P.scalar.wait_ge(SEM["dv"], dv_xsc[nu - 2])
                if nt == 0:
                    P.scalar.wait_ge(SEM["dv"], dv_scan_read)
                ts_ = slice(nt * 512, (nt + 1) * 512)
                P.scalar.activation(a_sb[:, ts_], p1[:], AF.Sigmoid, scale=-1.0)
                P.scalar.activation(beta[nu % 2][:], p1[:], AF.Sigmoid) \
                    .then_inc(*_i(cn, "ac", 1))
                ac_b = N["ac"]
                P.vector.wait_ge(SEM["ac"], ac_b)
                P.vector.tensor_tensor(xsc[:, ts_], beta[nu % 2][:], p2[:],
                                       ALU.mult).then_inc(*_i(cn, "dv", 1))
                dv_xsc[nu] = N["dv"]
            P.vector.drain()
            P.vector.tensor_tensor_scan(hxc[:], a_sb[:], xsc[:], 0.0,
                                        ALU.mult, ALU.add) \
                .then_inc(*_i(cn, "dv", 1))
            dv_scan_read = N["dv"]
            P.scalar.wait_ge(SEM["dv"], dv_scan_read)
            P.scalar.dma_start(out=hx_fm[s, j * 128:(j + 1) * 128, :], in_=hxc[:]) \
                .then_inc(*_i(cn, "sh", 16))
            sh_prev = N["sh"]
            P.vector.wait_ge(SEM["sh"], sh_prev)
        pe_seq_last = N["pe"]

    st.close()


def emit_l1_rest(nc, P, cn, dt):
    hx_fm, nf_fm, x1, y = dt["hx_fm"], dt["nf_fm"], dt["x1"], dt["y"]
    wts = {nm: dt[nm + "b"] for nm in ("w1t", "w2t", "w3t", "wrt", "wkt",
                                       "wvt2")}
    ffnw, mur, muk = dt["ffn_w"], dt["mu_r"], dt["mu_k"]
    ident, onescol, onesrow = dt["ident"], dt["ones_col"], dt["ones_row"]

    st = contextlib.ExitStack()
    sb = lambda nm, *a: st.enter_context(nc.sbuf_tensor("F_" + nm, *a))
    ps = lambda nm, *a: st.enter_context(nc.psum_tensor("F_" + nm, *a))
    hxs = sb("hxs", [128, NCH * T], F32)
    buf_hk = sb("buf_hk", [128, NCH * T], BF16)
    buf_gk = sb("buf_gk", [128, NCH * T], BF16)
    hnb, xkb = buf_hk, buf_hk
    glb, kab = buf_gk, buf_gk
    xrb = sb("xrb", [128, NCH * T], BF16)
    nfc = sb("nfc", [128, T], BF16)
    wta = [sb(f"wta{n}", [128, C], BF16) for n in range(2)]
    wtb = [sb(f"wtb{n}", [128, C], BF16) for n in range(2)]
    B3f = sb("B3f", [128, T], F32)
    B3 = B3f
    x1t = B3f[:, 0:C]
    outt = B3f[:, C:2 * C]
    tmu = sb("tmu", [128, T], BF16)
    sqs = [sb(f"sqs{n}", [128, 512], F32) for n in range(2)]
    s1b = [sb(f"s1b{n}", [128, 512], BF16) for n in range(2)]
    tmax = [sb(f"tmax{n}", [128, 512], F32) for n in range(2)]
    rb2 = [sb(f"rb2{n}", [128, 512], F32) for n in range(2)]
    t2b = [sb(f"t2b{n}", [128, 512], F32) for n in range(2)]
    msq = sb("msq", [1, 512], F32)
    rms3 = sb("rms3", [1, 512], F32)
    cvec = {nm: sb("c_" + nm, [128, NCH], F32)
            for nm in ("ffnw", "mur", "muk", "omr", "omk")}
    onesc = sb("onesc", [128, 1], F32)
    onesr = sb("onesr", [1, 128], F32)
    idf = sb("idf", [128, 128], F32)
    pA = ps("pA", [128, 512], F32)
    pB = ps("pB", [128, 512], F32)
    pms = ps("pms", [1, 512], F32)
    pbc = ps("pbc", [128, 512], F32)
    ptf = [ps(f"ptf{n}", [128, 512], F32) for n in range(2)]

    SEM, N = cn.sem, cn.n

    for src_, dst in ((ffnw, cvec["ffnw"]), (mur, cvec["mur"]), (muk, cvec["muk"]),
                      (ident, idf), (onescol, onesc), (onesrow, onesr)):
        P.sync.dma_start(out=dst[:], in_=src_[:]).then_inc(*_i(cn, "lc", 16))
    lc_all = N["lc"]
    P.vector.wait_ge(SEM["lc"], lc_all)
    P.scalar.wait_ge(SEM["lc"], lc_all)
    P.tensor.wait_ge(SEM["lc"], lc_all)
    P.vector.tensor_scalar(cvec["omr"][:], cvec["mur"][:], -1.0, 1.0,
                           ALU.mult, ALU.add)
    P.vector.tensor_scalar(cvec["omk"][:], cvec["muk"][:], -1.0, 1.0,
                           ALU.mult, ALU.add)
    P.vector.drain()

    wau = [0]; wa_free = {-1: 0, -2: 0}
    wbu = [0]; wb_free = {-1: 0, -2: 0}
    nfu = [0]; nf_free = {-1: 0, -2: 0}
    dv_fold = {}

    def load_wchunk(bufs, use_c, free_d, sems, wnm, j, fold):
        u = use_c[0]; use_c[0] += 1
        par = u % 2
        lws = sems[par]
        P.sync.wait_ge(SEM["pe"], free_d[u - 2])
        s3 = wts[wnm].rearrange("(dj p) c -> p dj c", p=128)[
            :, :, j * 128:(j + 1) * 128]
        d3 = bufs[par].ap().rearrange("p (dj c) -> p dj c", dj=NCH)
        P.sync.dma_start(out=d3, in_=s3).then_inc(*_i(cn, lws, 16))
        lw_t = N[lws]
        if fold:
            P.vector.wait_ge(SEM[lws], lw_t)
            for dj in range(NCH):
                ins = P.vector.tensor_scalar(
                    bufs[par][:, dj * 128:(dj + 1) * 128],
                    bufs[par][:, dj * 128:(dj + 1) * 128],
                    cvec["ffnw"][:, dj:dj + 1], None, ALU.mult)
            ins.then_inc(*_i(cn, "dv", 1))
            return par, lw_t, N["dv"]
        return par, lw_t, None

    sy_done = 0
    dv_out_read = 0
    squ = [0]
    pe_sq = {}
    pe_seq_end = 0
    dv_hxs_free = 0
    for s in range(SEQ):
        if s > 0:
            P.sync.wait_ge(SEM["dv"], dv_hxs_free)
            P.sync.wait_ge(SEM["pe"], pe_seq_end)
        P.sync.dma_start(out=hxs.ap().rearrange("p (j t) -> p j t", j=NCH),
                         in_=hx_fm[s].rearrange("(j p) t -> p j t", p=128)) \
            .then_inc(*_i(cn, "lh", 16))
        lh_t = N["lh"]
        P.vector.wait_ge(SEM["lh"], lh_t)
        P.scalar.wait_ge(SEM["lh"], lh_t)
        if s > 0:
            P.scalar.wait_ge(SEM["pe"], pe_seq_end)
            P.scalar.wait_ge(SEM["sy"], sy_done)
        for nt in range(4):
            ts_ = slice(nt * 512, (nt + 1) * 512)
            for j in range(NCH):
                u = squ[0]; squ[0] += 1
                if (u - 2) in pe_sq:
                    P.scalar.wait_ge(SEM["pe"], pe_sq[u - 2])
                P.scalar.activation(sqs[u % 2][:], hxs[:, j * T + nt * 512:
                                                       j * T + (nt + 1) * 512],
                                    AF.Square).then_inc(*_i(cn, "ac", 1))
                P.tensor.wait_ge(SEM["ac"], N["ac"])
                ins = P.tensor.matmul(pms[:], onesc[:], sqs[u % 2][:],
                                      start=(j == 0), stop=(j == NCH - 1))
                ins.then_inc(*_i(cn, "pe", 1))
                pe_sq[u] = N["pe"]
            pe_ms = N["pe"]
            P.vector.wait_ge(SEM["pe"], pe_ms)
            P.vector.drain()
            P.vector.tensor_scalar(msq[:], pms[:], 1.0 / C, EPS_RMS1,
                                   ALU.mult, ALU.add)
            P.vector.drain()
            P.vector.reciprocal(rms3[:], msq[:]).then_inc(*_i(cn, "dv", 1))
            dv_ms = N["dv"]
            P.tensor.wait_ge(SEM["dv"], dv_ms)
            P.scalar.wait_ge(SEM["dv"], dv_ms)
            P.scalar.activation(rms3[:], rms3[:], AF.Sqrt) \
                .then_inc(*_i(cn, "ac", 1))
            ac_r3 = N["ac"]
            P.tensor.wait_ge(SEM["ac"], ac_r3)
            if nt >= 1:
                P.tensor.wait_ge(SEM["ac"], ac_bc)
            ins = P.tensor.matmul(pbc[:], onesr[:], rms3[:], start=True, stop=True)
            ins.then_inc(*_i(cn, "pe", 1))
            pe_bc = N["pe"]
            P.scalar.wait_ge(SEM["pe"], pe_bc)
            P.scalar.activation(B3[:, ts_], pbc[:], AF.Copy) \
                .then_inc(*_i(cn, "ac", 1))
            ac_bc = N["ac"]
        ac_b3_done = N["ac"]
        P.vector.wait_ge(SEM["ac"], ac_b3_done)
        if s > 0:
            P.vector.wait_ge(SEM["pe"], pe_seq_end)
        for j in range(NCH):
            for nt in range(4):
                ins = P.vector.tensor_tensor(
                    hnb[:, j * T + nt * 512: j * T + (nt + 1) * 512],
                    hxs[:, j * T + nt * 512: j * T + (nt + 1) * 512],
                    B3[:, nt * 512:(nt + 1) * 512], ALU.mult)
        ins.then_inc(*_i(cn, "dv", 1))
        dv_hn = N["dv"]
        dv_gl = {}
        gu = [0]
        for j in range(NCH):
            para, lwa_t, dva = load_wchunk(wta, wau, wa_free, ("lwa0", "lwa1"),
                                           "w1t", j, True)
            parb, lwb_t, dvb = load_wchunk(wtb, wbu, wb_free, ("lwb0", "lwb1"),
                                           "w3t", j, True)
            P.tensor.wait_ge(SEM["dv"], max(dva, dvb, dv_hn))
            for nt in range(4):
                u = gu[0]; gu[0] += 1
                P.tensor.wait_ge(SEM["dv"], dv_gl.get(u - 1, 0))
                for dj in range(NCH):
                    ins = P.tensor.matmul(
                        pA[:], wta[para][:, dj * 128:(dj + 1) * 128],
                        hnb[:, dj * T + nt * 512: dj * T + (nt + 1) * 512],
                        start=(dj == 0), stop=(dj == NCH - 1))
                ins.then_inc(*_i(cn, "pe", 1))
                pe_1 = N["pe"]
                for dj in range(NCH):
                    ins = P.tensor.matmul(
                        pB[:], wtb[parb][:, dj * 128:(dj + 1) * 128],
                        hnb[:, dj * T + nt * 512: dj * T + (nt + 1) * 512],
                        start=(dj == 0), stop=(dj == NCH - 1))
                ins.then_inc(*_i(cn, "pe", 1))
                pe_3 = N["pe"]
                P.scalar.wait_ge(SEM["pe"], pe_1)
                if (u - 2) in dv_gl:
                    P.scalar.wait_ge(SEM["dv"], dv_gl[u - 2])
                P.scalar.activation(s1b[u % 2][:], pA[:], AF.Silu) \
                    .then_inc(*_i(cn, "ac", 1))
                ac_s1 = N["ac"]
                P.vector.wait_ge(SEM["ac"], ac_s1)
                P.vector.wait_ge(SEM["pe"], pe_3)
                P.vector.tensor_tensor(
                    glb[:, j * T + nt * 512: j * T + (nt + 1) * 512],
                    pB[:], s1b[u % 2][:], ALU.mult).then_inc(*_i(cn, "dv", 1))
                dv_gl[u] = N["dv"]
            wa_free[wau[0] - 1] = N["pe"]
            wb_free[wbu[0] - 1] = N["pe"]
        dv_gl_done = N["dv"]
        P.tensor.wait_ge(SEM["dv"], dv_gl_done)
        dv_s = {}
        su_ = [0]
        for dj in range(NCH):
            para, lwa_t, _ = load_wchunk(wta, wau, wa_free, ("lwa0", "lwa1"),
                                         "w2t", dj, False)
            P.tensor.wait_ge(SEM["lwa0" if para == 0 else "lwa1"], lwa_t)
            for nt in range(4):
                u = su_[0]; su_[0] += 1
                P.tensor.wait_ge(SEM["dv"], dv_s.get(u - 1, 0))
                for j in range(NCH):
                    ins = P.tensor.matmul(
                        pA[:], wta[para][:, j * 128:(j + 1) * 128],
                        glb[:, j * T + nt * 512: j * T + (nt + 1) * 512],
                        start=(j == 0), stop=(j == NCH - 1))
                ins.then_inc(*_i(cn, "pe", 1))
                pe_f = N["pe"]
                P.vector.wait_ge(SEM["pe"], pe_f)
                hsl = hxs[:, dj * T + nt * 512: dj * T + (nt + 1) * 512]
                P.vector.tensor_tensor(hsl, hsl, pA[:], ALU.add) \
                    .then_inc(*_i(cn, "dv", 1))
                dv_s[u] = N["dv"]
            wa_free[wau[0] - 1] = N["pe"]
        dv_s_done = N["dv"]
        pe_ff_end = N["pe"]
        P.vector.wait_ge(SEM["pe"], pe_ff_end)
        dv_nf = 0
        for j in range(NCH):
            u = nfu[0]; nfu[0] += 1
            lns = "ln0" if u % 2 == 0 else "ln1"
            P.sync.wait_ge(SEM["dv"], nf_free[u - 1])
            P.sync.dma_start(out=nfc[:],
                             in_=nf_fm[s, j * 128:(j + 1) * 128, :]) \
                .then_inc(*_i(cn, lns, 16))
            ln_t = N[lns]
            P.vector.wait_ge(SEM[lns], ln_t)
            for mu_c, om_c, dstb in ((cvec["mur"], cvec["omr"], xrb),
                                     (cvec["muk"], cvec["omk"], xkb)):
                P.vector.drain()
                P.vector.tensor_scalar(tmu[:], nfc[:], mu_c[:, j:j + 1],
                                       None, ALU.mult)
                P.vector.drain()
                P.vector.scalar_tensor_tensor(
                    dstb[:, j * T + 1:(j + 1) * T], nfc[:, 0:T - 1],
                    om_c[:, j:j + 1], tmu[:, 1:T], ALU.mult, ALU.add)
                ins = P.vector.tensor_copy(dstb[:, j * T:j * T + 1], tmu[:, 0:1])
            ins.then_inc(*_i(cn, "dv", 1))
            nf_free[u] = N["dv"]
        dv_xrk = N["dv"]
        P.tensor.wait_ge(SEM["dv"], dv_xrk)
        ku = [0]
        dv_ka = {}
        ac_ka = {}
        for dj in range(NCH):
            para, lwa_t, _ = load_wchunk(wta, wau, wa_free, ("lwa0", "lwa1"),
                                         "wkt", dj, False)
            P.tensor.wait_ge(SEM["lwa0" if para == 0 else "lwa1"], lwa_t)
            for nt in range(4):
                u = ku[0]; ku[0] += 1
                P.tensor.wait_ge(SEM["dv"], dv_ka.get(u - 1, 0))
                for j in range(NCH):
                    ins = P.tensor.matmul(
                        pA[:], wta[para][:, j * 128:(j + 1) * 128],
                        xkb[:, j * T + nt * 512: j * T + (nt + 1) * 512],
                        start=(j == 0), stop=(j == NCH - 1))
                ins.then_inc(*_i(cn, "pe", 1))
                pe_k = N["pe"]
                P.vector.wait_ge(SEM["pe"], pe_k)
                if (u - 2) in ac_ka:
                    P.vector.wait_ge(SEM["ac"], ac_ka[u - 2])
                P.vector.tensor_scalar(tmax[u % 2][:], pA[:], 0.0, None,
                                       ALU.max).then_inc(*_i(cn, "dv", 1))
                dv_ka[u] = N["dv"]
                if u == 0:
                    P.scalar.wait_ge(SEM["pe"], pe_ff_end)
                P.scalar.wait_ge(SEM["dv"], dv_ka[u])
                P.scalar.activation(kab[:, dj * T + nt * 512:
                                        dj * T + (nt + 1) * 512],
                                    tmax[u % 2][:], AF.Square) \
                    .then_inc(*_i(cn, "ac", 1))
                ac_ka[u] = N["ac"]
            wa_free[wau[0] - 1] = N["pe"]
        ac_ka_done = N["ac"]
        P.tensor.wait_ge(SEM["ac"], ac_ka_done)
        vu = [0]
        dv_v = {}
        ac_r = {}
        for dj in range(NCH):
            para, lwa_t, _ = load_wchunk(wta, wau, wa_free, ("lwa0", "lwa1"),
                                         "wrt", dj, False)
            parb, lwb_t, _ = load_wchunk(wtb, wbu, wb_free, ("lwb0", "lwb1"),
                                         "wvt2", dj, False)
            P.tensor.wait_ge(SEM["lwa0" if para == 0 else "lwa1"], lwa_t)
            P.tensor.wait_ge(SEM["lwb0" if parb == 0 else "lwb1"], lwb_t)
            for nt in range(4):
                u = vu[0]; vu[0] += 1
                P.tensor.wait_ge(SEM["dv"], dv_v.get(u - 1, 0))
                for j in range(NCH):
                    ins = P.tensor.matmul(
                        pA[:], wta[para][:, j * 128:(j + 1) * 128],
                        xrb[:, j * T + nt * 512: j * T + (nt + 1) * 512],
                        start=(j == 0), stop=(j == NCH - 1))
                ins.then_inc(*_i(cn, "pe", 1))
                pe_r = N["pe"]
                for j in range(NCH):
                    ins = P.tensor.matmul(
                        pB[:], wtb[parb][:, j * 128:(j + 1) * 128],
                        kab[:, j * T + nt * 512: j * T + (nt + 1) * 512],
                        start=(j == 0), stop=(j == NCH - 1))
                ins.then_inc(*_i(cn, "pe", 1))
                pe_v = N["pe"]
                P.scalar.wait_ge(SEM["pe"], pe_r)
                if (u - 2) in dv_v:
                    P.scalar.wait_ge(SEM["dv"], dv_v[u - 2])
                P.scalar.activation(rb2[u % 2][:], pA[:], AF.Sigmoid) \
                    .then_inc(*_i(cn, "ac", 1))
                ac_r[u] = N["ac"]
                P.vector.wait_ge(SEM["ac"], ac_r[u])
                P.vector.wait_ge(SEM["pe"], pe_v)
                P.vector.tensor_tensor(t2b[u % 2][:], pB[:], rb2[u % 2][:],
                                       ALU.mult)
                P.vector.drain()
                hsl = hxs[:, dj * T + nt * 512: dj * T + (nt + 1) * 512]
                P.vector.tensor_tensor(hsl, hsl, t2b[u % 2][:], ALU.add) \
                    .then_inc(*_i(cn, "dv", 1))
                dv_v[u] = N["dv"]
            wa_free[wau[0] - 1] = N["pe"]
            wb_free[wbu[0] - 1] = N["pe"]
        dv_final_s = N["dv"]
        dv_hxs_free = dv_final_s
        P.tensor.wait_ge(SEM["dv"], dv_final_s)
        dv_out_read = max(dv_out_read, dv_final_s)
        for i in range(NTT):
            P.sync.wait_ge(SEM["dv"], dv_out_read)
            P.sync.dma_start(out=x1t, in_=x1[s, 128 * i:128 * (i + 1), :]) \
                .then_inc(*_i(cn, "lx1", 16))
            lx1_t = N["lx1"]
            if i > 0:
                P.tensor.wait_ge(SEM["dv"], dv_out)
            for half in range(2):
                for djj in range(4):
                    dj = half * 4 + djj
                    ins = P.tensor.transpose(
                        ptf[half][:, djj * 128:(djj + 1) * 128],
                        hxs[:, dj * T + 128 * i: dj * T + 128 * (i + 1)], idf[:])
                    if djj == 3:
                        ins.then_inc(*_i(cn, "pe", 1))
            pe_tp = N["pe"]
            P.vector.wait_ge(SEM["pe"], pe_tp)
            P.vector.wait_ge(SEM["lx1"], lx1_t)
            P.vector.wait_ge(SEM["sy"], sy_done)
            P.vector.tensor_tensor(outt[:, 0:512], x1t[:, 0:512], ptf[0][:],
                                   ALU.add)
            P.vector.tensor_tensor(outt[:, 512:1024], x1t[:, 512:1024], ptf[1][:],
                                   ALU.add).then_inc(*_i(cn, "dv", 1))
            dv_out = N["dv"]
            dv_out_read = dv_out
            P.scalar.wait_ge(SEM["dv"], dv_out)
            P.scalar.dma_start(out=y[s, 128 * i:128 * (i + 1), :], in_=outt) \
                .then_inc(*_i(cn, "sy", 16))
            sy_done = N["sy"]
        pe_seq_end = N["pe"]

    st.close()


def _i(cn, nm, v):
    cn.n[nm] += v
    return cn.sem[nm], v


def _run(nc, in_maps, **kw):
    return run_bass_kernel_spmd(nc, in_maps, core_ids=list(range(len(in_maps))), **kw)


ALL_SEMS = ["pre0", "preSq", "preB", "pre1",
            "ld0", "ld1", "gc", "ga0", "ga1", "gw", "gf",
            "lc", "lx", "lw0", "lw1", "la0", "la1", "sa0", "sa1",
            "lb", "la", "lg", "lv0", "lv1", "sa", "lw", "sh",
            "lwa0", "lwa1", "lwb0", "lwb1", "lh", "ln0", "ln1", "lx1", "sy", "lz",
            "dv", "ac", "pe"]

STORE_SEMS = ["ga0", "ga1", "gw", "gf", "sa0", "sa1", "sa", "sh", "sy"]


def _declare_drams(nc):
    dt = {}
    EI, IN, EO = "ExternalInput", "Internal", "ExternalOutput"

    def d(name, shape, dtype, kind):
        dt[name] = nc.dram_tensor(name, shape, dtype, kind=kind).ap()
    d("x", [SEQ, T, C], F32, EI)
    d("cent", [128, NCH * ER], F32, EI)
    d("ident", [128, 128], F32, EI)
    d("rope_cs", [128, T], BF16, EI)
    d("rope_sn", [128, T], BF16, EI)
    d("perm_swap", [128, 128], BF16, EI)
    for nm in ("wq1L", "wk1L", "wq2L", "wk2L"):
        d(nm, [8, 128, NCH * 128], BF16, IN)
    d("wvL", [2, 128, NCH * 512], BF16, IN)
    d("masks", [4, 128, 512], BF16, EI)
    d("tri128", [128, 128], BF16, EI)
    for nm in SQUARE_WTS:
        d(nm, [C, C], F32, EI)
        d(nm + "b", [C, C], BF16, IN)
    d("w1s", [ES, C, HF], F32, EI)
    d("w2s", [ES, HF, C], F32, EI)
    d("w1r", [ER, C, HF], F32, EI)
    d("w2r", [ER, HF, C], F32, EI)
    d("w1Ls", [ES, 8, 128, NCH * 512], BF16, IN)
    d("w2Ls", [ES, 8, 128, 4 * C], BF16, IN)
    d("w1Lr", [ER, 8, 128, NCH * 512], BF16, IN)
    d("w2Lr", [ER, 8, 128, 4 * C], BF16, IN)
    d("b1s", [128, ES * 32], F32, EI)
    d("b1r", [128, ER * 32], F32, EI)
    d("b2s", [1, ES * C], F32, EI)
    d("b2r", [1, ER * C], F32, EI)
    d("b2sbb", [1, ES * C], BF16, IN)
    d("b2rbb", [1, ER * C], BF16, IN)
    for nm in ("g_s", "g_r", "ln_w", "ffn_w", "mu_r", "mu_k"):
        d(nm, [128, NCH], F32, EI)
    d("ones_row", [1, 128], F32, EI)
    d("ones_col", [128, 1], F32, EI)
    d("d_onesb", [1, 128], BF16, IN)
    d("d_identb", [128, 128], BF16, IN)
    d("d_centb", [128, NCH * ER], BF16, IN)
    d("xn_fm", [SEQ, C, T], BF16, IN)
    d("xnr_fm", [SEQ, C, T], BF16, IN)
    d("acc0", [SEQ, T, C], F32, IN)
    d("acc1", [SEQ, T, C], F32, IN)
    d("wg", [SEQ, T, ER], F32, IN)
    d("x1", [SEQ, T, C], F32, IN)
    d("na_fm", [SEQ, C, T], BF16, IN)
    d("nf_fm", [SEQ, C, T], BF16, IN)
    d("hx_fm", [SEQ, C, T], F32, IN)
    d("y", [SEQ, T, C], F32, EO)
    if _DBG:
        d("dbg_y", [SEQ, 4, 128, 8 * 512], BF16, EO)
        d("dbg_k1", [SEQ, 128, 8 * T], BF16, EO)
        d("dbg_k2", [SEQ, 128, 8 * T], BF16, EO)
        d("dbg_v", [SEQ, 128, NTT * C], BF16, EO)
        d("dbg_q", [8, 128, 1024], BF16, EO)
    return dt


SQUARE_WTS = ("wq1t", "wk1t", "wq2t", "wk2t", "wvt", "woutt",
              "wgt", "wct", "w1t", "w2t", "w3t", "wrt", "wkt", "wvt2")


def emit_prestage(nc, P, cn, dt):
    for src_, dst in (("ident", "d_identb"), ("cent", "d_centb"),
                      ("ones_row", "d_onesb"), ("b2s", "b2sbb"), ("b2r", "b2rbb")):
        P.gpsimd.dma_start(out=dt[dst], in_=dt[src_]).then_inc(*_i(cn, "pre0", 16))
    for nm in SQUARE_WTS:
        if nm in ("wq1t", "wk1t", "wq2t", "wk2t", "wvt", "woutt"):
            continue
        P.gpsimd.dma_start(out=dt[nm + "b"], in_=dt[nm]) \
            .then_inc(*_i(cn, "preSq", 16))
    for ne, w1f, w2f, w1d, w2d in ((ES, "w1s", "w2s", "w1Ls", "w2Ls"),
                                   (ER, "w1r", "w2r", "w1Lr", "w2Lr")):
        for e in range(ne):
            w1v = dt[w1f][e].rearrange("(j p) (hb hl) -> hb p j hl",
                                       p=128, hl=512)
            w2v = dt[w2f][e].rearrange("(g q p) c -> g p q c", q=4, p=128)
            for hb in range(8):
                d1 = dt[w1d][e, hb].rearrange("p (j hl) -> p j hl", hl=512)
                P.gpsimd.dma_start(out=d1, in_=w1v[hb]) \
                    .then_inc(*_i(cn, "pre1", 16))
                d2 = dt[w2d][e, hb].rearrange("p (q c) -> p q c", c=C)
                P.gpsimd.dma_start(out=d2, in_=w2v[hb]) \
                    .then_inc(*_i(cn, "pre1", 16))
    P.sync.wait_ge(cn.sem["pre0"], cn.n["pre0"])


def emit_preB(nc, P, cn, dt):
    st = contextlib.ExitStack()
    sb = lambda nm, *a: st.enter_context(nc.sbuf_tensor("PB_" + nm, *a))
    f32t = [sb(f"f{n}", [128, 4096], F32) for n in range(4)]
    bf16t = [sb(f"b{n}", [128, 4096], BF16) for n in range(4)]
    SEM, N = cn.sem, cn.n
    for Q in (P.sync, P.scalar):
        for nm in ("dv", "ac", "pe", "ld0", "ld1", "ga0", "ga1", "gw", "gf"):
            Q.wait_ge(SEM[nm], N[nm])
    jobs = []
    for srcn, dstn in (("wk1t", "wk1L"), ("wk2t", "wk2L"),
                       ("wq1t", "wq1L"), ("wq2t", "wq2L")):
        v = dt[srcn].rearrange("(cj p) (hp c) -> p hp cj c", p=128, c=128)
        for h4 in range(2):
            dst = dt[dstn][h4 * 4:(h4 + 1) * 4].rearrange("hp p f -> p hp f")
            jobs.append(([(v[:, h4 * 4 + hh], 8, 128) for hh in range(4)],
                         dst))
    vv = dt["wvt"].rearrange("(cj p) (ha c) -> p ha cj c", p=128, c=512)
    for ha in range(2):
        dst = dt["wvL"][ha:ha + 1].rearrange("ha p f -> p ha f")
        jobs.append(([(vv[:, ha], 8, 512)], dst))
    wo = dt["woutt"].rearrange("(hp p) c -> p hp c", p=128)
    wob = dt["wouttb"].rearrange("(hp p) c -> p hp c", p=128)
    for h4 in range(2):
        jobs.append(([(wo[:, h4 * 4 + hh], 1, 1024) for hh in range(4)],
                     wob[:, h4 * 4:(h4 + 1) * 4]))
    ac_done = {}
    stq = {}
    ld_done = {}
    scr = sb("scr", [128, 512], BF16)
    nj = len(jobs)
    def issue_load(i):
        srcs, dst3 = jobs[i]
        par = i % 4
        if i - 4 >= 0:
            P.sync.wait_ge(SEM["preB"], stq[i - 4])
        w = 4096 // len(srcs)
        for si, (s3, jj, cc) in enumerate(srcs):
            fv = f32t[par][:, si * w:(si + 1) * w] \
                .rearrange("p (j c) -> p j c", j=jj)
            P.sync.dma_start(out=fv, in_=s3).then_inc(*_i(cn, "lc", 16))
        ld_done[i] = N["lc"]

    issue_load(0)
    issue_load(1)
    for i in range(nj):
        srcs, dst3 = jobs[i]
        par = i % 4
        a = dst3.shape[1]
        bv = bf16t[par].ap().rearrange("p (a f) -> p a f", a=a)
        if i + 2 < nj:
            issue_load(i + 2)
        else:
            P.sync.dma_start(out=f32t[(i + 2) % 4][:, 0:64],
                             in_=dt["tri128"].bitcast(F32)[:, 0:64]) \
                .then_inc(*_i(cn, "lc", 16))
            ld_done[i + 2] = N["lc"]
        P.scalar.wait_ge(SEM["lc"], ld_done[i + 1])
        P.scalar.activation(bf16t[par][:], f32t[par][:], AF.Copy) \
            .then_inc(*_i(cn, "ac", 1))
        P.scalar.activation(scr[:], bf16t[par][:, 0:512], AF.Copy) \
            .then_inc(*_i(cn, "ac", 1))
        ac_done[i] = N["ac"]
        P.scalar.wait_ge(SEM["ac"], ac_done[i])
        P.scalar.dma_start(out=dst3, in_=bv).then_inc(*_i(cn, "preB", 16))
        stq[i] = N["preB"]
    st.close()


def _barrier(P, cn, sems=None):
    names = ["dv", "ac", "pe"] + STORE_SEMS if sems is None else sems
    for eng in (P.sync, P.tensor, P.vector, P.scalar, P.gpsimd):
        for nm in names:
            eng.wait_ge(cn.sem[nm], cn.n[nm])


def build_fused():
    nc = bass.Bass("TRN2", target_bir_lowering=False)
    P = Prog(nc)
    dt = _declare_drams(nc)
    cn = Counters(nc, ALL_SEMS)
    emit_prestage(nc, P, cn, dt)
    emit_l0_prep(nc, P, cn, dt)
    emit_preB(nc, P, cn, dt)
    _barrier(P, cn)
    for eng in (P.sync, P.tensor, P.vector, P.scalar):
        eng.wait_ge(cn.sem["preB"], cn.n["preB"])
    emit_l0_attn(nc, P, cn, dt)
    _barrier(P, cn)
    P.sync.wait_ge(cn.sem["pre1"], cn.n["pre1"])
    emit_l0_ffn(nc, P, cn, dt)
    _barrier(P, cn)
    emit_l1_prep(nc, P, cn, dt)
    _barrier(P, cn)
    emit_l1_scan(nc, P, cn, dt)
    _barrier(P, cn)
    emit_l1_rest(nc, P, cn, dt)
    _barrier(P, cn)
    P.finish()
    return nc


def kernel(**inputs):
    ip = _host_prep(inputs)
    nc = build_fused()
    ims = []
    for c in range(NCORE):
        im = {"x": ip["x"][c]}
        for nm in ("cent", "ident", "rope_cs", "rope_sn", "perm_swap", "masks",
                   "tri128",
                   "w1s", "w2s", "w1r", "w2r", "b1s", "b1r", "b2s", "b2r",
                   "g_s", "g_r", "ln_w", "ffn_w", "mu_r", "mu_k",
                   "ones_row", "ones_col") + SQUARE_WTS:
            im[nm] = ip[nm]
        ims.append(im)
    res = _run(nc, ims)
    return np.concatenate([res.results[c]["y"] for c in range(NCORE)],
                          0).reshape(16, T, C)


# version 13
# speedup vs baseline: 1.0484x; 1.0484x over previous
import contextlib
import numpy as np
import ml_dtypes
import concourse.bass as bass
import concourse.mybir as mybir
from concourse.bass_utils import run_bass_kernel_spmd

F32 = mybir.dt.float32
BF16 = mybir.dt.bfloat16
AF = mybir.ActivationFunctionType
ALU = mybir.AluOpType
AX = mybir.AxisListType
BF = ml_dtypes.bfloat16

NCORE = 8
SEQ = 2
T = 2048
C = 1024
NCH = C // 128
NTT = T // 128
NH = 16
HD = 64
HF = 4096
NHCH = HF // 128
ER, ES = 8, 2
TOPK = 2
EPS_LN = 1e-5
EPS_RMS0 = float(np.finfo(np.float32).eps)
EPS_RMS1 = 1e-6
INV_D2 = 1.0 / (HD * HD)
_DBG = False


class _Call:
    __slots__ = ("meth", "a", "k", "incs")

    def __init__(self, meth, a, k):
        self.meth, self.a, self.k = meth, a, k
        self.incs = []

    def then_inc(self, sem, v):
        self.incs.append((sem, v))
        return self

    def replay(self, eng):
        ins = getattr(eng, self.meth)(*self.a, **self.k)
        for s, v in self.incs:
            ins = ins.then_inc(s, v)


class _EngRec:
    def __init__(self, prog, name):
        self._p, self._n = prog, name

    def __getattr__(self, meth):
        def call(*a, **k):
            c = _Call(meth, a, k)
            self._p.q[self._n].append(c)
            return c
        return call


class Prog:
    ENGS = ("sync", "tensor", "vector", "scalar", "gpsimd")

    def __init__(self, nc):
        self.nc = nc
        self.q = {e: [] for e in self.ENGS}
        for e in self.ENGS:
            setattr(self, e, _EngRec(self, e))

    def finish(self):
        q = self.q
        with self.nc.Block() as block:
            @block.sync
            def _(eng):
                for c in q["sync"]:
                    c.replay(eng)

            @block.tensor
            def _(eng):
                for c in q["tensor"]:
                    c.replay(eng)

            @block.vector
            def _(eng):
                for c in q["vector"]:
                    c.replay(eng)

            @block.scalar
            def _(eng):
                for c in q["scalar"]:
                    c.replay(eng)

            @block.gpsimd
            def _(eng):
                for c in q["gpsimd"]:
                    c.replay(eng)


class Counters:

    def __init__(self, nc, names):
        self.stack = contextlib.ExitStack()
        self.sem = {}
        self.n = {}
        for nm in names:
            self.sem[nm] = self.stack.enter_context(nc.semaphore(nm))
        self.n = {nm: 0 for nm in names}

    def inc(self, nm, v=1):
        self.n[nm] += v
        return self.sem[nm], self.n[nm]


def _host_prep(inputs):
    ip = {}
    x = np.ascontiguousarray(inputs["L0_attn_x"])
    ip["x"] = x.reshape(NCORE, SEQ, T, C)
    for nm, key in [("wq1t", "L0_attn_Wq1"), ("wk1t", "L0_attn_Wk1"),
                    ("wq2t", "L0_attn_Wq2"), ("wk2t", "L0_attn_Wk2"),
                    ("wvt", "L0_attn_Wv"), ("woutt", "L0_attn_Wout"),
                    ("wgt", "L1_attn_Wg"), ("wct", "L1_attn_Wc"),
                    ("w1t", "L1_attn_w1"), ("w2t", "L1_attn_w2"),
                    ("w3t", "L1_attn_w3"), ("wrt", "L1_ffn_W_r"),
                    ("wkt", "L1_ffn_W_k"), ("wvt2", "L1_ffn_W_v")]:
        ip[nm] = np.ascontiguousarray(inputs[key].T)
    for nm, key in [("g_s", "L0_ffn_g_shared"), ("g_r", "L0_ffn_g_routed"),
                    ("ln_w", "L1_attn_ln_w"), ("ffn_w", "L1_attn_ffn_w"),
                    ("mu_r", "L1_ffn_mu_r"), ("mu_k", "L1_ffn_mu_k")]:
        ip[nm] = np.ascontiguousarray(inputs[key].reshape(NCH, 128).T)
    ip["cent"] = np.ascontiguousarray(
        inputs["L0_ffn_centroids"].reshape(NCH, 128, ER).transpose(1, 0, 2)
        .reshape(128, NCH * ER))
    ip["w1s"] = np.ascontiguousarray(inputs["L0_ffn_W1_s"])
    ip["w2s"] = np.ascontiguousarray(inputs["L0_ffn_W2_s"])
    ip["w1r"] = np.ascontiguousarray(inputs["L0_ffn_W1_r"])
    ip["w2r"] = np.ascontiguousarray(inputs["L0_ffn_W2_r"])
    ip["b1s"] = np.ascontiguousarray(
        inputs["L0_ffn_b1_s"].reshape(ES, NHCH, 128).transpose(2, 0, 1)
        .reshape(128, ES * NHCH))
    ip["b1r"] = np.ascontiguousarray(
        inputs["L0_ffn_b1_r"].reshape(ER, NHCH, 128).transpose(2, 0, 1)
        .reshape(128, ER * NHCH))
    ip["b2s"] = np.ascontiguousarray(inputs["L0_ffn_b2_s"].reshape(1, ES * C))
    ip["b2r"] = np.ascontiguousarray(inputs["L0_ffn_b2_r"].reshape(1, ER * C))
    f = np.arange(0, HD, 2, dtype=np.float32) / HD
    inv_freq = (10000.0 ** -f)
    tt = np.arange(T, dtype=np.float32)
    freqs = tt[None, :] * inv_freq[:, None]
    cosb = np.cos(freqs).astype(BF).astype(np.float32)
    sinb = np.sin(freqs).astype(BF).astype(np.float32)
    cs64 = np.concatenate([cosb, cosb], 0)
    sn64 = np.concatenate([sinb, -sinb], 0)
    ip["rope_cs"] = np.ascontiguousarray(
        np.concatenate([cs64, cs64], 0).astype(BF))
    ip["rope_sn"] = np.ascontiguousarray(
        np.concatenate([sn64, sn64], 0).astype(BF))
    pm = np.zeros((128, 128), np.float32)
    for mm_ in range(128):
        pm[(mm_ % 64 + 32) % 64 + 64 * (mm_ // 64), mm_] = 1.0
    ip["perm_swap"] = np.ascontiguousarray(pm.astype(BF))
    kk = np.arange(128, dtype=np.int64)[:, None]
    qq = np.arange(512, dtype=np.int64)[None, :]
    masks = np.stack([(kk + 128 * j <= qq).astype(np.float32) * INV_D2
                      for j in range(4)])
    ip["masks"] = masks.astype(BF)
    ip["tri128"] = np.ascontiguousarray(
        (kk <= qq[:, 0:128]).astype(np.float32).astype(BF))
    ip["ident"] = np.eye(128, dtype=np.float32)
    ip["ones_row"] = np.ones((1, 128), dtype=np.float32)
    ip["ones_col"] = np.ones((128, 1), dtype=np.float32)
    return ip


def emit_l0_prep(nc, P, cn, dt):
    x = dt["x"]
    xn_fm, xnr_fm = dt["xn_fm"], dt["xnr_fm"]
    acc0, wg = dt["acc0"], dt["wg"]

    st = contextlib.ExitStack()
    sb = lambda nm, *a: st.enter_context(nc.sbuf_tensor("A_" + nm, *a))
    ps = lambda nm, *a: st.enter_context(nc.psum_tensor("A_" + nm, *a))
    xt = [sb(f"xt{b}", [128, C], F32) for b in range(2)]
    xn_tm = [sb(f"xntm{b}", [128, C], BF16) for b in range(2)]
    xnr_tm = [sb(f"xnrtm{b}", [128, C], BF16) for b in range(2)]
    acc_sb = [sb(f"accsb{b}", [128, C], F32) for b in range(2)]
    bs = sb("bs", [128, 12], F32)
    mv = sb("mv", [128, 2 * NTT], F32)
    stats = {nm: sb("st_" + nm, [128, NTT], F32)
             for nm in ("vp", "rvp", "re", "rre", "rstd", "r2", "c2", "rp1", "mr")}
    eps0_t = sb("eps0t", [128, 1], F32)
    idb = sb("idb", [128, 128], BF16)
    centb = sb("centb", [128, NCH * ER], BF16)
    xnf_sb = sb("xnf", [128, NCH * T], BF16)
    xnrf_sb = sb("xnrf", [128, NCH * T], BF16)
    sg = sb("sg", [128, ER], F32)
    gsc = {nm: sb("g_" + nm, [128, 1], F32)
           for nm in ("top1", "top2", "den", "rec")}
    eqm = sb("eqm", [128, ER], F32)
    smask = sb("smask", [128, ER], F32)
    gemask = sb("gemask", [128, ER], F32)
    wsc = sb("wsc", [128, ER], F32)
    wgt_sb = sb("wgt_sb", [128, ER], F32)
    pt = [ps(f"pt{b}", [128, 1024], BF16) for b in range(2)]
    pg = ps("pg", [128, ER], F32)

    SEM, N = cn.sem, cn.n
    P.sync.dma_start(out=idb[:], in_=dt["d_identb"][:]).then_inc(*_i(cn, "gc", 16))
    P.sync.dma_start(out=centb[:], in_=dt["d_centb"][:]).then_inc(*_i(cn, "gc", 16))
    gd_const = N["gc"]
    P.vector.memset(eps0_t[:], EPS_RMS0)

    dv_g = None
    gd_wg = None
    dv_done = {}
    pe_done = {}
    ac_evac = {}
    gd_acc = {}
    for s in range(SEQ):
        for i in range(NTT):
            g = s * NTT + i
            b = g % 2
            mvc = mv[:, 2 * i:2 * i + 2]
            mean = mv[:, 2 * i:2 * i + 1]
            var = mv[:, 2 * i + 1:2 * i + 2]
            col = lambda nm: stats[nm][:, i:i + 1]
            lds = "ld0" if b == 0 else "ld1"
            if g >= 2:
                P.sync.wait_ge(SEM["dv"], dv_done[g - 2])
            P.sync.dma_start(out=xt[b][:], in_=x[s, 128 * i:128 * (i + 1), :]) \
                .then_inc(*_i(cn, lds, 16))
            ld_x = N[lds]
            P.vector.wait_ge(SEM[lds], ld_x)
            P.vector.bn_stats(bs[:, 0:6], xt[b][:, 0:512])
            P.vector.bn_stats(bs[:, 6:12], xt[b][:, 512:1024])
            P.vector.drain()
            P.vector.bn_aggr(mvc, bs[:])
            P.vector.drain()
            P.vector.tensor_scalar(col("vp"), var, EPS_LN, None, ALU.add)
            P.vector.drain()
            P.vector.reciprocal(col("rvp"), col("vp"))
            P.vector.drain()
            P.vector.scalar_tensor_tensor(col("re"), var, col("rvp"), eps0_t[:],
                                          ALU.mult, ALU.add)
            P.vector.drain()
            P.vector.reciprocal(col("rre"), col("re")).then_inc(*_i(cn, "dv", 1))
            dv_pre = N["dv"]
            P.scalar.wait_ge(SEM["dv"], dv_pre)
            P.scalar.activation(col("rstd"), col("rvp"), AF.Sqrt)
            P.scalar.activation(col("r2"), col("rre"), AF.Sqrt) \
                .then_inc(*_i(cn, "ac", 1))
            ac_sqrt = N["ac"]
            P.vector.wait_ge(SEM["ac"], ac_sqrt)
            if g >= 2:
                P.vector.wait_ge(SEM["pe"], pe_done[g - 2])
                P.vector.wait_ge(SEM["ga0" if b == 0 else "ga1"], gd_acc[g - 2])
            P.vector.tensor_tensor(col("c2"), col("rstd"), col("r2"), ALU.mult)
            P.vector.tensor_scalar(col("rp1"), col("rstd"), 1.0, None, ALU.add)
            P.vector.tensor_tensor(col("mr"), mean, col("rstd"), ALU.mult)
            P.vector.drain()
            P.vector.tensor_scalar(xn_tm[b][:], xt[b][:], mean, col("rstd"),
                                   ALU.subtract, ALU.mult)
            P.vector.tensor_scalar(xnr_tm[b][:], xt[b][:], mean, col("c2"),
                                   ALU.subtract, ALU.mult)
            P.vector.tensor_scalar(acc_sb[b][:], xt[b][:], col("rp1"), col("mr"),
                                   ALU.mult, ALU.subtract) \
                .then_inc(*_i(cn, "dv", 1))
            dv_done[g] = N["dv"]
            gas = "ga0" if b == 0 else "ga1"
            P.scalar.wait_ge(SEM["dv"], dv_done[g])
            P.scalar.dma_start(out=acc0[s, 128 * i:128 * (i + 1), :],
                               in_=acc_sb[b][:]).then_inc(*_i(cn, gas, 16))
            gd_acc[g] = N[gas]
            P.tensor.wait_ge(SEM["dv"], dv_done[g])
            if g >= 1:
                P.tensor.wait_ge(SEM["ac"], ac_evac[g - 1])
            if g == 0:
                P.tensor.wait_ge(SEM["gc"], gd_const)
            for j in range(NCH):
                ins = P.tensor.transpose(pt[0][:, 128 * j:128 * (j + 1)],
                                         xn_tm[b][:, 128 * j:128 * (j + 1)], idb[:])
                if j == NCH - 1:
                    ins.then_inc(*_i(cn, "pe", 1))
            pe_xn = N["pe"]
            for j in range(NCH):
                ins = P.tensor.transpose(pt[1][:, 128 * j:128 * (j + 1)],
                                         xnr_tm[b][:, 128 * j:128 * (j + 1)], idb[:])
                if j == NCH - 1:
                    ins.then_inc(*_i(cn, "pe", 1))
            pe_done[g] = N["pe"]
            pe_xnr = N["pe"]
            xnf_v = xnf_sb.ap().rearrange("p (j t) -> p j t", j=NCH)
            xnrf_v = xnrf_sb.ap().rearrange("p (j t) -> p j t", j=NCH)
            pt0_v = pt[0].ap().rearrange("p (j t) -> p j t", j=NCH)
            pt1_v = pt[1].ap().rearrange("p (j t) -> p j t", j=NCH)
            P.scalar.wait_ge(SEM["pe"], pe_xn)
            P.scalar.activation(xnf_v[:, :, 128 * i:128 * (i + 1)], pt0_v[:, :, :],
                                AF.Copy)
            P.scalar.wait_ge(SEM["pe"], pe_xnr)
            P.scalar.activation(xnrf_v[:, :, 128 * i:128 * (i + 1)], pt1_v[:, :, :],
                                AF.Copy).then_inc(*_i(cn, "ac", 1))
            ac_evac[g] = N["ac"]
        ac_seq_done = N["ac"]
        hbm_xn = xn_fm[s].rearrange("(j p) t -> p j t", p=128)
        hbm_xnr = xnr_fm[s].rearrange("(j p) t -> p j t", p=128)
        sb_xn = xnf_sb.ap().rearrange("p (j t) -> p j t", j=NCH)
        sb_xnr = xnrf_sb.ap().rearrange("p (j t) -> p j t", j=NCH)
        P.scalar.wait_ge(SEM["ac"], ac_seq_done)
        P.scalar.dma_start(out=hbm_xn, in_=sb_xn).then_inc(*_i(cn, "gf", 16))
        P.scalar.dma_start(out=hbm_xnr, in_=sb_xnr).then_inc(*_i(cn, "gf", 16))
        gd_fm = N["gf"]
        P.tensor.wait_ge(SEM["ac"], ac_seq_done)
        for i in range(NTT):
            for j in range(NCH):
                ins = P.tensor.matmul(pg[:],
                                      xnf_sb[:, j * T + 128 * i: j * T + 128 * (i + 1)],
                                      centb[:, ER * j:ER * (j + 1)],
                                      start=(j == 0), stop=(j == NCH - 1))
                if j == NCH - 1:
                    ins.then_inc(*_i(cn, "pe", 1))
            pe_g = N["pe"]
            P.scalar.wait_ge(SEM["pe"], pe_g)
            if dv_g is not None:
                P.scalar.wait_ge(SEM["dv"], dv_g)
            P.scalar.activation(sg[:], pg[:], AF.Sigmoid).then_inc(*_i(cn, "ac", 1))
            ac_g = N["ac"]
            P.vector.wait_ge(SEM["ac"], ac_g)
            if gd_wg is not None:
                P.vector.wait_ge(SEM["gw"], gd_wg)
            P.vector.reduce_max(gsc["top1"][:], sg[:], axis=AX.X)
            P.vector.reduce_sum(gsc["den"][:], sg[:], axis=AX.X)
            P.vector.drain()
            P.vector.tensor_scalar(eqm[:], sg[:], gsc["top1"][:], None, ALU.is_ge)
            P.vector.reciprocal(gsc["rec"][:], gsc["den"][:])
            P.vector.drain()
            P.vector.scalar_tensor_tensor(smask[:], eqm[:], -1e30, sg[:],
                                          ALU.mult, ALU.add)
            P.vector.drain()
            P.vector.reduce_max(gsc["top2"][:], smask[:], axis=AX.X)
            P.vector.drain()
            P.vector.tensor_scalar(gemask[:], sg[:], gsc["top2"][:], None, ALU.is_ge)
            P.vector.tensor_scalar(wsc[:], sg[:], gsc["rec"][:], None, ALU.mult)
            P.vector.drain()
            P.vector.tensor_tensor(wgt_sb[:], wsc[:], gemask[:], ALU.mult) \
                .then_inc(*_i(cn, "dv", 1))
            dv_g = N["dv"]
            P.tensor.wait_ge(SEM["ac"], ac_g)
            P.scalar.wait_ge(SEM["dv"], dv_g)
            P.scalar.dma_start(out=wg[s, 128 * i:128 * (i + 1), :], in_=wgt_sb[:]) \
                .then_inc(*_i(cn, "gw", 16))
            gd_wg = N["gw"]
        if s + 1 < SEQ:
            pe_gates = N["pe"]
            P.scalar.wait_ge(SEM["pe"], pe_gates)
            P.scalar.wait_ge(SEM["gf"], gd_fm)

    st.close()


def emit_l0_attn(nc, P, cn, dt):
    xn_fm, acc0, acc1 = dt["xn_fm"], dt["acc0"], dt["acc1"]
    rcs, rsn = dt["rope_cs"], dt["rope_sn"]
    perm, tri = dt["perm_swap"], dt["tri128"]

    st = contextlib.ExitStack()
    sb = lambda nm, *a: st.enter_context(nc.sbuf_tensor("B_" + nm, *a))
    ps = lambda nm, *a: st.enter_context(nc.psum_tensor("B_" + nm, *a))
    xnf = sb("xnf", [128, NCH * T], BF16)
    k1b = sb("k1b", [128, 8 * T], BF16)
    k2b = sb("k2b", [128, 8 * T], BF16)
    vb = sb("vb", [128, NTT * C], BF16)
    wkq = [sb(f"wkq{n}", [128, NCH * 128], BF16) for n in range(4)]
    wvb = sb("wvb", [128, NCH * 512], BF16)
    wog = sb("wog", [128, 8 * C], BF16)
    qp = [sb(f"qp{n}", [128, 2 * 512], BF16) for n in range(2)]
    kfb = [sb(f"kfb{n}", [128, 512], BF16) for n in range(2)]
    rt = [sb(f"rt{n}", [128, 512], F32) for n in range(2)]
    e1b = [sb(f"e1b{n}", [128, 512], BF16) for n in range(2)]
    e2b = [sb(f"e2b{n}", [128, 512], BF16) for n in range(2)]
    patb = [sb(f"patb{n}", [128, 512], BF16) for n in range(4)]
    ysb = sb("ysb", [128, 8 * 512], BF16)
    acct = [sb(f"acct{n}", [128, 512], F32) for n in range(2)]
    csb = sb("csb", [128, T], BF16)
    snb = sb("snb", [128, T], BF16)
    permb = sb("permb", [128, 128], BF16)
    trib = sb("trib", [128, 128], BF16)
    fsc = sb("fsc", [128, 1], F32)
    facs = sb("facs", [128, 1], F32)
    pq = [ps(f"pq{n}", [128, 512], F32) for n in range(2)]
    ss = [ps(f"ss{n}", [128, 512], F32) for n in range(4)]
    py = ps("py", [128, 512], F32)
    po = ps("po", [128, 512], F32)

    SEM, N = cn.sem, cn.n

    P.sync.dma_start(out=csb[:], in_=rcs[:]).then_inc(*_i(cn, "lc", 16))
    P.sync.dma_start(out=snb[:], in_=rsn[:]).then_inc(*_i(cn, "lc", 16))
    P.sync.dma_start(out=permb[:], in_=perm[:]).then_inc(*_i(cn, "lc", 16))
    P.sync.dma_start(out=trib[:], in_=tri[:]).then_inc(*_i(cn, "lc", 16))
    P.sync.dma_start(out=wog.ap().rearrange("p (hp c) -> p hp c", c=C),
                     in_=dt["wouttb"].rearrange("(hp p) c -> p hp c", p=128)) \
        .then_inc(*_i(cn, "lc", 16))
    lc_all = N["lc"]
    P.vector.wait_ge(SEM["lc"], lc_all)
    P.tensor.wait_ge(SEM["lc"], lc_all)
    P.scalar.wait_ge(SEM["lc"], lc_all)
    P.vector.memset(fsc[:], 1.0)

    wku = [0]; wk_free = {}; wk_loaded = {}

    def load_wslice(srcL, idx):
        u = wku[0]; wku[0] += 1
        b = u % 4
        lws = "lw0" if b % 2 == 0 else "lw1"
        P.sync.wait_ge(SEM["pe"], wk_free.get(u - 4, 0))
        P.sync.dma_start(out=wkq[b][:], in_=srcL[idx]) \
            .then_inc(*_i(cn, lws, 16))
        wk_loaded[u] = (lws, N[lws])
        return u, b

    pqu = [0]; pq_free = {}
    ssu = [0]; ss_free = {}
    patu = [0]; pat_free = {}
    e1u = [0]; e1_free = {}
    rope_done = {}
    kf_evac = {}
    pend = [None]

    def chunk_proj(wb, nslc, kdst_ap, tslc, wwait, kdst_pe=None):
        u = pqu[0]; pqu[0] += 1
        pb = pq[u % 2]
        P.tensor.wait_ge(SEM[wwait[0]], wwait[1])
        if (u - 2) in pq_free:
            P.tensor.wait_ge(SEM["ac"], pq_free[u - 2])
        for cj in range(NCH):
            ins = P.tensor.matmul(
                pb[:], wb[:, cj * 128:(cj + 1) * 128],
                xnf[:, cj * T + nslc.start: cj * T + nslc.stop],
                start=(cj == 0), stop=(cj == NCH - 1))
        ins.then_inc(*_i(cn, "pe", 1))
        pe_proj = N["pe"]
        P.scalar.wait_ge(SEM["pe"], pe_proj)
        if (u - 2) in rope_done:
            P.scalar.wait_ge(SEM["dv"], rope_done[u - 2])
        P.scalar.activation(kfb[u % 2][:], pb[:], AF.Copy) \
            .then_inc(*_i(cn, "ac", 1))
        pq_free[u] = N["ac"]
        P.scalar.activation(facs[:], fsc[:], AF.Copy) \
            .then_inc(*_i(cn, "ac", 1))
        kf_evac[u] = N["ac"]
        if pend[0] is not None:
            swap_rope(*pend[0])
        pend[0] = (u, kdst_ap, tslc, kdst_pe)
        return u

    def swap_rope(u, kdst_ap, tslc, kdst_pe):
        su = ssu[0]; ssu[0] += 1
        psw = ss[su % 4]
        if (su - 4) in ss_free:
            d_, a_ = ss_free[su - 4]
            P.tensor.wait_ge(SEM["dv"], d_)
            P.tensor.wait_ge(SEM["ac"], a_)
        P.tensor.wait_ge(SEM["ac"], kf_evac[u])
        P.tensor.matmul(psw[:], permb[:], kfb[u % 2][:],
                        start=True, stop=True).then_inc(*_i(cn, "pe", 1))
        pe_sw = N["pe"]
        P.vector.wait_ge(SEM["pe"], pe_sw)
        P.vector.drain()
        P.vector.tensor_tensor(rt[0][:], kfb[u % 2][:], csb[:, tslc], ALU.mult)
        P.vector.tensor_tensor(rt[1][:], psw[:], snb[:, tslc], ALU.mult)
        P.vector.drain()
        if kdst_pe is not None:
            P.vector.wait_ge(SEM["pe"], kdst_pe)
        P.vector.tensor_tensor(kdst_ap, rt[0][:], rt[1][:], ALU.add) \
            .then_inc(*_i(cn, "dv", 1))
        rope_done[u] = N["dv"]
        ss_free[su] = (N["dv"], N["ac"])

    def flush_pend():
        if pend[0] is not None:
            swap_rope(*pend[0])
            pend[0] = None

    pe_seq_end = 0
    pcc = [0]
    pe_smm_qp = {}
    dv_q_ready = {}
    ac_ysb = {}
    dv_acct = {}
    sa_acct = {}
    pe_oproj_end = [0]
    for s in range(SEQ):
        if s > 0:
            P.sync.wait_ge(SEM["pe"], pe_seq_end)
        pre0 = load_wslice(dt["wk1L"], 0)
        pre1_ = load_wslice(dt["wk2L"], 0)
        P.sync.dma_start(out=xnf.ap().rearrange("p (j t) -> p j t", j=NCH),
                         in_=xn_fm[s].rearrange("(j p) t -> p j t", p=128)) \
            .then_inc(*_i(cn, "lx", 16))
        lx_t = N["lx"]
        P.sync.dma_start(out=trib[:], in_=tri[:]).then_inc(*_i(cn, "lz", 16))
        lz_t = N["lz"]
        first = [True]
        for hp in range(8):
            for wL, kdst in ((dt["wk1L"], k1b), (dt["wk2L"], k2b)):
                if hp == 0:
                    u_w, b = pre0 if kdst is k1b else pre1_
                else:
                    u_w, b = load_wslice(wL, hp)
                for nt in range(4):
                    ts_ = slice(nt * 512, (nt + 1) * 512)
                    if first[0]:
                        P.tensor.wait_ge(SEM["lz"], lz_t)
                        P.tensor.wait_ge(SEM["lx"], lx_t)
                        first[0] = False
                    chunk_proj(wkq[b], ts_,
                               k1b[:, hp * T + nt * 512:
                                   hp * T + (nt + 1) * 512] if kdst is k1b
                               else k2b[:, hp * T + nt * 512:
                                        hp * T + (nt + 1) * 512],
                               ts_, wk_loaded[u_w])
                wk_free[u_w] = N["pe"]
        for ha in range(2):
            if ha == 1:
                P.sync.wait_ge(SEM["pe"], N["pe"])
            P.sync.dma_start(out=wvb[:], in_=dt["wvL"][ha]) \
                .then_inc(*_i(cn, "lv0", 16))
            lv_t = N["lv0"]
            P.tensor.wait_ge(SEM["lv0"], lv_t)
            for i in range(NTT):
                su = ssu[0]; ssu[0] += 1
                pv = ss[su % 4]
                if (su - 4) in ss_free:
                    d_, a_ = ss_free[su - 4]
                    P.tensor.wait_ge(SEM["dv"], d_)
                    P.tensor.wait_ge(SEM["ac"], a_)
                for cj in range(NCH):
                    ins = P.tensor.matmul(
                        pv[:], xnf[:, cj * T + i * 128: cj * T + (i + 1) * 128],
                        wvb[:, cj * 512:(cj + 1) * 512],
                        start=(cj == 0), stop=(cj == NCH - 1))
                ins.then_inc(*_i(cn, "pe", 1))
                P.scalar.wait_ge(SEM["pe"], N["pe"])
                P.scalar.activation(
                    vb[:, i * C + ha * 512: i * C + (ha + 1) * 512],
                    pv[:], AF.Copy, scale=INV_D2).then_inc(*_i(cn, "ac", 1))
                ss_free[su] = (N["dv"], N["ac"])
        P.scalar.activation(facs[:], fsc[:], AF.Copy).then_inc(*_i(cn, "ac", 1))
        ac_v_done = N["ac"]
        flush_pend()
        P.vector.tensor_scalar(fsc[:], fsc[:], 1.0, None, ALU.mult) \
            .then_inc(*_i(cn, "dv", 1))
        dv_k_done = N["dv"]
        first_smm = [True]

        qstash = {}

        def prefetch_q(qt, hp):
            qstash[(qt, hp)] = [load_wslice(wL, hp)
                                for wL in (dt["wq1L"], dt["wq2L"])]

        def build_q(qt, hp):
            pc = pcc[0]; pcc[0] += 1
            qcols = slice(qt * 512 + 0, qt * 512 + 512)
            for wi, (u_w, b) in enumerate(qstash.pop((qt, hp))):
                chunk_proj(wkq[b], qcols,
                           qp[pc % 2][:, wi * 512:(wi + 1) * 512],
                           qcols, wk_loaded[u_w],
                           kdst_pe=pe_smm_qp.get(pc - 2, 0))
                wk_free[u_w] = N["pe"]
            return pc

        def nxt_of(qt, hp):
            if hp < 7:
                return (qt, hp + 1)
            return (qt + 1, 0) if qt < 3 else None

        prefetch_q(0, 0)
        prefetch_q(0, 1)
        qpars = {(0, 0): build_q(0, 0)}
        for qt in range(4):
            nkt = 4 * qt + 4
            for hp in range(8):
                nxt = nxt_of(qt, hp)
                if nxt is not None:
                    nxt2 = nxt_of(*nxt)
                    if nxt2 is not None:
                        prefetch_q(*nxt2)
                    qpars[nxt] = build_q(*nxt)
                else:
                    flush_pend()
                P.vector.tensor_scalar(fsc[:], fsc[:], 1.0, None, ALU.mult) \
                    .then_inc(*_i(cn, "dv", 1))
                dv_q_ready[(qt, hp)] = N["dv"]
                pc = qpars[(qt, hp)]
                qpar = pc % 2
                for hh in range(2):
                    h = 2 * hp + hh
                    info = {}
                    pes = {}
                    ac_e1 = {}
                    dv_pat_after = {}

                    def emit_smm(kt):
                        su = ssu[0]; ssu[0] += 2
                        s1p, s2p = ss[su % 4], ss[(su + 1) % 4]
                        for du in (su - 4, su - 3):
                            if du in ss_free:
                                d_, a_ = ss_free[du]
                                P.tensor.wait_ge(SEM["dv"], d_)
                                P.tensor.wait_ge(SEM["ac"], a_)
                        if first_smm[0]:
                            P.tensor.wait_ge(SEM["dv"], dv_k_done)
                            first_smm[0] = False
                        if kt == 0 and hh == 0:
                            P.tensor.wait_ge(SEM["dv"], dv_q_ready[(qt, hp)])
                        ins = P.tensor.matmul(
                            s1p[:],
                            k1b[64 * hh:64 * hh + 64,
                                hp * T + kt * 128: hp * T + (kt + 1) * 128],
                            qp[qpar][64 * hh:64 * hh + 64, 0:512],
                            start=True, stop=True)
                        ins.then_inc(*_i(cn, "pe", 1))
                        ins = P.tensor.matmul(
                            s2p[:],
                            k2b[64 * hh:64 * hh + 64,
                                hp * T + kt * 128: hp * T + (kt + 1) * 128],
                            qp[qpar][64 * hh:64 * hh + 64, 512:1024],
                            start=True, stop=True)
                        ins.then_inc(*_i(cn, "pe", 1))
                        pes[kt] = (su, N["pe"])

                    def emit_e1(kt):
                        eu = e1u[0]; e1u[0] += 1
                        su, pe_s = pes[kt]
                        P.scalar.wait_ge(SEM["pe"], pe_s)
                        if (eu - 2) in e1_free:
                            P.scalar.wait_ge(SEM["dv"], e1_free[eu - 2])
                        P.scalar.activation(e1b[eu % 2][:], ss[su % 4][:],
                                            AF.Copy) \
                            .then_inc(*_i(cn, "ac", 1))
                        P.vector.wait_ge(SEM["pe"], pe_s)
                        if (eu - 2) in e1_free:
                            P.vector.wait_ge(SEM["pe"],
                                             pat_free.get(eu - 2, 0))
                        P.vector.tensor_copy(e2b[eu % 2][:],
                                             ss[(su + 1) % 4][:]) \
                            .then_inc(*_i(cn, "dv", 1))
                        ac_e1[kt] = (eu, N["ac"], N["dv"])
                        ss_free[su] = (N["dv"], N["ac"])
                        ss_free[su + 1] = (N["dv"], N["ac"])

                    def emit_patm(kt, fence):
                        pu = patu[0]; patu[0] += 1
                        eu = ac_e1[kt][0]
                        r = pu % 4
                        P.vector.wait_ge(SEM["ac"], fence[0])
                        if (pu - 4) in pat_free:
                            P.vector.wait_ge(SEM["pe"], pat_free[pu - 4])
                        lo = 0
                        if kt >= 4 * qt:
                            j = kt - 4 * qt
                            lo = 128 * j
                            P.vector.drain()
                            P.vector.tensor_tensor(
                                e1b[eu % 2][:, lo:lo + 128],
                                e1b[eu % 2][:, lo:lo + 128], trib[:], ALU.mult)
                            P.vector.drain()
                        P.vector.tensor_tensor(patb[r][:, lo:512],
                                               e2b[eu % 2][:, lo:512],
                                               e1b[eu % 2][:, lo:512],
                                               ALU.mult) \
                            .then_inc(*_i(cn, "dv", 1))
                        e1_free[eu] = N["dv"]
                        dv_pat_after[kt] = N["dv"]
                        info[kt] = (r, lo, pu)

                    def emit_ymm(kt, dv_need):
                        r, lo, pu = info[kt]
                        P.tensor.wait_ge(SEM["dv"], dv_need)
                        if kt == 0 and hh == 0:
                            P.tensor.wait_ge(SEM["ac"], ac_v_done)
                            P.tensor.wait_ge(SEM["ac"], ac_ysb.get(hp - 1, 0))
                            if hp == 0:
                                P.tensor.wait_ge(SEM["dv"],
                                                 dv_acct.get(1, 0))
                        ins = P.tensor.matmul(
                            py[64 * hh:64 * hh + 64, lo:512],
                            vb[:, kt * C + h * 64: kt * C + h * 64 + 64],
                            patb[r][:, lo:512],
                            start=(kt == 0), stop=(kt == nkt - 1))
                        ins.then_inc(*_i(cn, "pe", 1))
                        pat_free[pu] = N["pe"]

                    for kt in range(nkt):
                        emit_smm(kt)
                        emit_e1(kt)
                        if kt >= 1:
                            emit_patm(kt - 1, (ac_e1[kt][1], ac_e1[kt][2]))
                        if kt >= 3:
                            emit_ymm(kt - 3, dv_pat_after[kt - 2])
                    P.scalar.activation(facs[:], fsc[:], AF.Copy) \
                        .then_inc(*_i(cn, "ac", 1))
                    emit_patm(nkt - 1, (N["ac"], N["dv"]))
                    P.vector.tensor_scalar(fsc[:], fsc[:], 1.0, None,
                                           ALU.mult).then_inc(*_i(cn, "dv", 1))
                    dv_tail = N["dv"]
                    if nkt >= 3:
                        emit_ymm(nkt - 3, dv_pat_after[nkt - 2])
                    emit_ymm(nkt - 2, dv_tail)
                    emit_ymm(nkt - 1, dv_tail)
                pe_smm_qp[pc] = N["pe"]
                P.scalar.wait_ge(SEM["pe"], N["pe"])
                if hp == 0:
                    P.scalar.wait_ge(SEM["pe"], pe_oproj_end[0])
                P.scalar.activation(ysb[:, hp * 512:(hp + 1) * 512], py[:],
                                    AF.Copy)
                P.scalar.activation(facs[:], fsc[:], AF.Copy) \
                    .then_inc(*_i(cn, "ac", 1))
                ac_ysb[hp] = N["ac"]
            P.tensor.wait_ge(SEM["ac"], ac_ysb[7])
            for t8 in range(8):
                sl, ha = t8 // 2, t8 % 2
                bpar = t8 % 2
                row0 = qt * 512 + sl * 128
                csl = slice(ha * 512, (ha + 1) * 512)
                las = "la0" if bpar == 0 else "la1"
                sas = "sa0" if bpar == 0 else "sa1"
                if bpar in sa_acct:
                    P.sync.wait_ge(SEM[sas], sa_acct[bpar])
                P.sync.dma_start(out=acct[bpar][:],
                                 in_=acc0[s, row0:row0 + 128, csl]) \
                    .then_inc(*_i(cn, las, 16))
                la_t = N[las]
                if bpar in dv_acct:
                    P.tensor.wait_ge(SEM["dv"], dv_acct[bpar])
                pob = po if bpar == 0 else py
                for hp in range(8):
                    ins = P.tensor.matmul(
                        pob[:],
                        ysb[:, hp * 512 + sl * 128: hp * 512 + sl * 128 + 128],
                        wog[:, hp * C + ha * 512: hp * C + (ha + 1) * 512],
                        start=(hp == 0), stop=(hp == 7))
                ins.then_inc(*_i(cn, "pe", 1))
                pe_o = N["pe"]
                P.vector.wait_ge(SEM["pe"], pe_o)
                P.vector.wait_ge(SEM[las], la_t)
                P.vector.drain()
                P.vector.tensor_tensor(acct[bpar][:], acct[bpar][:], pob[:],
                                       ALU.add)
                P.vector.tensor_scalar(fsc[:], fsc[:], 1.0, None, ALU.mult) \
                    .then_inc(*_i(cn, "dv", 1))
                dv_acct[bpar] = N["dv"]
                P.scalar.wait_ge(SEM["dv"], dv_acct[bpar])
                P.scalar.dma_start(out=acc1[s, row0:row0 + 128, csl],
                                   in_=acct[bpar][:]) \
                    .then_inc(*_i(cn, sas, 16))
                sa_acct[bpar] = N[sas]
            pe_oproj_end[0] = N["pe"]
        pe_seq_end = N["pe"]

    st.close()


def emit_l0_ffn(nc, P, cn, dt):
    xnr_fm, acc1, wg, x1 = dt["xnr_fm"], dt["acc1"], dt["wg"], dt["x1"]
    b1s, b1r = dt["b1s"], dt["b1r"]
    g_s, g_r = dt["g_s"], dt["g_r"]

    st = contextlib.ExitStack()
    sb = lambda nm, *a: st.enter_context(nc.sbuf_tensor("C_" + nm, *a))
    ps = lambda nm, *a: st.enter_context(nc.psum_tensor("C_" + nm, *a))
    xb = [sb(f"xb{n}", [128, NCH * 512], BF16) for n in range(2)]
    xgs = [sb(f"xgs{n}", [128, NCH * 512], BF16) for n in range(2)]
    xgr = [sb(f"xgr{n}", [128, NCH * 512], BF16) for n in range(2)]
    accs = [sb(f"accs{n}", [128, 4 * C], F32) for n in range(2)]
    wgt4 = [sb(f"wgt4{n}", [128, 4 * ER], F32) for n in range(2)]
    hfm = [sb(f"hfm{n}", [128, NHCH * 512], BF16) for n in range(2)]
    w1t = [sb(f"w1t{n}", [128, NCH * 512], BF16) for n in range(2)]
    w2t = [sb(f"w2t{n}", [128, 4 * C], BF16) for n in range(2)]
    b1sb = sb("b1sb", [128, ES * 32], F32)
    b1rb = sb("b1rb", [128, ER * 32], F32)
    b2sb = sb("b2sb", [1, ES * C], BF16)
    b2rb = sb("b2rb", [1, ER * C], BF16)
    gsb = sb("gsb", [128, NCH], F32)
    grb = sb("grb", [128, NCH], F32)
    onesb = sb("onesb", [1, 128], BF16)
    pout = [ps(f"pout{n}", [128, 512], F32) for n in range(8)]

    SEM, N = cn.sem, cn.n

    P.sync.dma_start(out=b1sb[:], in_=b1s[:]).then_inc(*_i(cn, "lb", 16))
    P.sync.dma_start(out=b1rb[:], in_=b1r[:]).then_inc(*_i(cn, "lb", 16))
    P.sync.dma_start(out=gsb[:], in_=g_s[:]).then_inc(*_i(cn, "lb", 16))
    P.sync.dma_start(out=grb[:], in_=g_r[:]).then_inc(*_i(cn, "lb", 16))
    P.sync.dma_start(out=b2sb[:], in_=dt["b2sbb"][:]).then_inc(*_i(cn, "lb", 16))
    P.sync.dma_start(out=b2rb[:], in_=dt["b2rbb"][:]).then_inc(*_i(cn, "lb", 16))
    P.sync.dma_start(out=onesb[:], in_=dt["d_onesb"][:]).then_inc(*_i(cn, "lb", 16))
    lb_all = N["lb"]
    P.vector.wait_ge(SEM["lb"], lb_all)
    P.scalar.wait_ge(SEM["lb"], lb_all)
    P.tensor.wait_ge(SEM["lb"], lb_all)
    P.sync.wait_ge(SEM["pre1"], cn.n["pre1"])

    w1u = [0]; w1_free = {-1: 0, -2: 0}
    w2u = [0]; w2_free = {-1: 0, -2: 0}
    lx_tb, la_tb, lg_tb, dv_xg = {}, {}, {}, {}
    sa_tb = {-1: 0, -2: 0}
    pe_tb_end = {-1: 0, -2: 0}
    dv_acc_end = {-1: 0, -2: 0}
    dv_bank = {b: 0 for b in range(8)}
    pe_out_exp = {}
    ecnt = [0]

    NTB = SEQ * 4
    GROUPS = ((ES, (dt["w1Ls"], dt["w2Ls"]), b1sb, b2sb, xgs),
              (ER, (dt["w1Lr"], dt["w2Lr"]), b1rb, b2rb, xgr))

    def emit_tb_loads(k):
        s, tb = k // 4, k % 4
        par = k % 2
        P.sync.wait_ge(SEM["dv"], dv_xg.get(k - 2, 0))
        hb3 = xnr_fm[s].rearrange("(j p) t -> p j t", p=128)[
            :, :, tb * 512:(tb + 1) * 512]
        xb3 = xb[par].ap().rearrange("p (j t) -> p j t", j=NCH)
        P.sync.dma_start(out=xb3, in_=hb3).then_inc(*_i(cn, "lx", 16))
        lx_tb[k] = N["lx"]
        P.sync.wait_ge(SEM["sa"], sa_tb[k - 2])
        ac3 = acc1[s, tb * 512:(tb + 1) * 512, :] \
            .rearrange("(tt p) c -> p tt c", p=128)
        as3 = accs[par].ap().rearrange("p (tt c) -> p tt c", tt=4)
        P.sync.dma_start(out=as3, in_=ac3).then_inc(*_i(cn, "la", 16))
        la_tb[k] = N["la"]
        P.sync.wait_ge(SEM["dv"], dv_acc_end[k - 2])
        wg3 = wg[s, tb * 512:(tb + 1) * 512, :] \
            .rearrange("(tt p) e -> p tt e", p=128)
        wg4 = wgt4[par].ap().rearrange("p (tt e) -> p tt e", tt=4)
        P.sync.dma_start(out=wg4, in_=wg3).then_inc(*_i(cn, "lg", 16))
        lg_tb[k] = N["lg"]

    def emit_xg(k):
        par = k % 2
        P.vector.wait_ge(SEM["lx"], lx_tb[k])
        P.vector.wait_ge(SEM["pe"], pe_tb_end[k - 2])
        for gt, dst in ((gsb, xgs[par]), (grb, xgr[par])):
            for j in range(NCH):
                ins = P.vector.tensor_scalar(
                    dst[:, j * 512:(j + 1) * 512],
                    xb[par][:, j * 512:(j + 1) * 512],
                    gt[:, j:j + 1], None, ALU.mult)
        ins.then_inc(*_i(cn, "dv", 1))
        dv_xg[k] = N["dv"]

    emit_tb_loads(0)
    emit_xg(0)
    for k in range(NTB):
        s, tb = k // 4, k % 4
        par = k % 2
        for grp, (ne, (w1L, w2L), b1t, b2t, xgb) in enumerate(GROUPS):
            xg = xgb[par]
            for e in range(ne):
                ec = ecnt[0]; ecnt[0] += 1
                epar = ec % 2
                ac_gelu = {}
                for hb in range(8):
                    u = w1u[0]; w1u[0] += 1
                    parw = u % 2
                    P.sync.wait_ge(SEM["pe"], w1_free[u - 2])
                    P.sync.dma_start(out=w1t[parw][:], in_=w1L[e, hb]) \
                        .then_inc(*_i(cn, "lw0", 16))
                    lw_t = N["lw0"]
                    P.tensor.wait_ge(SEM["lw0"], lw_t)
                    for hc in range(4):
                        hg = hb * 4 + hc
                        pb = pout[6 + (hg & 1)]
                        if hg < 2:
                            P.tensor.wait_ge(SEM["dv"], dv_bank[6 + hg])
                            if hg == 0:
                                P.tensor.wait_ge(SEM["dv"], dv_xg[k])
                        else:
                            P.tensor.wait_ge(SEM["ac"], ac_gelu[hg - 2])
                        for cj in range(NCH):
                            ins = P.tensor.matmul(
                                pb[:],
                                w1t[parw][:, cj * 512 + hc * 128:
                                          cj * 512 + (hc + 1) * 128],
                                xg[:, cj * 512:(cj + 1) * 512],
                                start=(cj == 0), stop=(cj == NCH - 1))
                        ins.then_inc(*_i(cn, "pe", 1))
                        pe_h = N["pe"]
                        P.scalar.wait_ge(SEM["pe"], pe_h)
                        if hg == 0:
                            P.scalar.wait_ge(SEM["pe"],
                                             pe_out_exp.get(ec - 2, 0))
                        P.scalar.activation(
                            hfm[epar][:, hg * 512:(hg + 1) * 512], pb[:],
                            AF.Gelu,
                            bias=b1t[:, e * 32 + hg:e * 32 + hg + 1]) \
                            .then_inc(*_i(cn, "ac", 1))
                        ac_gelu[hg] = N["ac"]
                    w1_free[u] = N["pe"]
                ac_h_done = N["ac"]
                P.tensor.wait_ge(SEM["ac"], ac_h_done)
                for b8 in range(8):
                    tt, ch = b8 // 2, b8 % 2
                    P.tensor.wait_ge(SEM["dv"], dv_bank[b8])
                    ins = P.tensor.matmul(
                        pout[b8][:], onesb[:],
                        b2t[:, e * C + ch * 512:e * C + (ch + 1) * 512],
                        start=True, stop=False)
                for g32 in range(8):
                    u = w2u[0]; w2u[0] += 1
                    parv = u % 2
                    P.sync.wait_ge(SEM["pe"], w2_free[u - 2])
                    P.sync.dma_start(out=w2t[parv][:], in_=w2L[e, g32]) \
                        .then_inc(*_i(cn, "lv0", 16))
                    lv_t = N["lv0"]
                    P.tensor.wait_ge(SEM["lv0"], lv_t)
                    for q in range(4):
                        hc32 = g32 * 4 + q
                        for b8 in range(8):
                            tt, ch = b8 // 2, b8 % 2
                            ins = P.tensor.matmul(
                                pout[b8][:],
                                hfm[epar][:, hc32 * 512 + tt * 128:
                                          hc32 * 512 + (tt + 1) * 128],
                                w2t[parv][:, q * C + ch * 512:
                                          q * C + (ch + 1) * 512],
                                start=False, stop=(hc32 == NHCH - 1))
                        ins.then_inc(*_i(cn, "pe", 1))
                    w2_free[u] = N["pe"]
                pe_out = N["pe"]
                pe_out_exp[ec] = pe_out
                P.vector.wait_ge(SEM["pe"], pe_out)
                P.vector.wait_ge(SEM["la"], la_tb[k])
                P.vector.wait_ge(SEM["lg"], lg_tb[k])
                P.vector.drain()
                for b8 in (6, 7, 0, 1, 2, 3, 4, 5):
                    tt, ch = b8 // 2, b8 % 2
                    asl = accs[par][:, tt * C + ch * 512:
                                    tt * C + (ch + 1) * 512]
                    if grp == 0:
                        ins = P.vector.tensor_tensor(asl, asl, pout[b8][:],
                                                     ALU.add)
                    else:
                        ins = P.vector.scalar_tensor_tensor(
                            asl, pout[b8][:],
                            wgt4[par][:, tt * ER + e: tt * ER + e + 1],
                            asl, ALU.mult, ALU.add)
                    ins.then_inc(*_i(cn, "dv", 1))
                    dv_bank[b8] = N["dv"]
                dv_acc_end[k] = N["dv"]
            if grp == 0 and k + 1 < NTB:
                emit_tb_loads(k + 1)
                emit_xg(k + 1)
        P.scalar.wait_ge(SEM["dv"], dv_acc_end[k])
        x13 = x1[s, tb * 512:(tb + 1) * 512, :] \
            .rearrange("(tt p) c -> p tt c", p=128)
        as3 = accs[par].ap().rearrange("p (tt c) -> p tt c", tt=4)
        P.scalar.dma_start(out=x13, in_=as3).then_inc(*_i(cn, "sa", 16))
        sa_tb[k] = N["sa"]
        pe_tb_end[k] = N["pe"]

    st.close()


def emit_l1_prep(nc, P, cn, dt):
    x = dt["x1"]
    na_fm, nf_fm = dt["na_fm"], dt["nf_fm"]

    st = contextlib.ExitStack()
    sb = lambda nm, *a: st.enter_context(nc.sbuf_tensor("D_" + nm, *a))
    ps = lambda nm, *a: st.enter_context(nc.psum_tensor("D_" + nm, *a))
    xt = [sb(f"xt{n}", [128, C], F32) for n in range(2)]
    na_tm = [sb(f"natm{n}", [128, C], BF16) for n in range(2)]
    nf_tm = [sb(f"nftm{n}", [128, C], BF16) for n in range(2)]
    bs = sb("bs", [128, 12], F32)
    mv = sb("mv", [128, 2 * NTT], F32)
    stats = {nm: sb("st_" + nm, [128, NTT], F32)
             for nm in ("vp", "rvp", "re", "rre", "rstd", "r2", "c2")}
    eps1_t = sb("eps1t", [128, 1], F32)
    idb = sb("idb", [128, 128], BF16)
    naf_sb = sb("naf", [128, NCH * T], BF16)
    nff_sb = sb("nff", [128, NCH * T], BF16)
    pt = [ps(f"pt{n}", [128, 1024], BF16) for n in range(2)]

    SEM, N = cn.sem, cn.n
    P.sync.dma_start(out=idb[:], in_=dt["d_identb"][:]).then_inc(*_i(cn, "gc", 16))
    gd_const = N["gc"]
    P.vector.memset(eps1_t[:], EPS_RMS1)

    dv_done = {}
    pe_done = {}
    ac_evac = {}
    for s in range(SEQ):
        for i in range(NTT):
            g = s * NTT + i
            b = g % 2
            mean = mv[:, 2 * i:2 * i + 1]
            var = mv[:, 2 * i + 1:2 * i + 2]
            col = lambda nm: stats[nm][:, i:i + 1]
            lds = "ld0" if b == 0 else "ld1"
            if g >= 2:
                P.sync.wait_ge(SEM["dv"], dv_done[g - 2])
            P.sync.dma_start(out=xt[b][:], in_=x[s, 128 * i:128 * (i + 1), :]) \
                .then_inc(*_i(cn, lds, 16))
            ld_x = N[lds]
            P.vector.wait_ge(SEM[lds], ld_x)
            P.vector.bn_stats(bs[:, 0:6], xt[b][:, 0:512])
            P.vector.bn_stats(bs[:, 6:12], xt[b][:, 512:1024])
            P.vector.drain()
            P.vector.bn_aggr(mv[:, 2 * i:2 * i + 2], bs[:])
            P.vector.drain()
            P.vector.tensor_scalar(col("vp"), var, EPS_LN, None, ALU.add)
            P.vector.drain()
            P.vector.reciprocal(col("rvp"), col("vp"))
            P.vector.drain()
            P.vector.scalar_tensor_tensor(col("re"), var, col("rvp"), eps1_t[:],
                                          ALU.mult, ALU.add)
            P.vector.drain()
            P.vector.reciprocal(col("rre"), col("re")).then_inc(*_i(cn, "dv", 1))
            dv_pre = N["dv"]
            P.scalar.wait_ge(SEM["dv"], dv_pre)
            P.scalar.activation(col("rstd"), col("rvp"), AF.Sqrt)
            P.scalar.activation(col("r2"), col("rre"), AF.Sqrt) \
                .then_inc(*_i(cn, "ac", 1))
            ac_sqrt = N["ac"]
            P.vector.wait_ge(SEM["ac"], ac_sqrt)
            if g >= 2:
                P.vector.wait_ge(SEM["pe"], pe_done[g - 2])
            P.vector.tensor_tensor(col("c2"), col("rstd"), col("r2"), ALU.mult)
            P.vector.drain()
            P.vector.tensor_scalar(nf_tm[b][:], xt[b][:], mean, col("rstd"),
                                   ALU.subtract, ALU.mult)
            P.vector.tensor_scalar(na_tm[b][:], xt[b][:], mean, col("c2"),
                                   ALU.subtract, ALU.mult) \
                .then_inc(*_i(cn, "dv", 1))
            dv_done[g] = N["dv"]
            P.tensor.wait_ge(SEM["dv"], dv_done[g])
            if g >= 1:
                P.tensor.wait_ge(SEM["ac"], ac_evac[g - 1])
            if g == 0:
                P.tensor.wait_ge(SEM["gc"], gd_const)
            for j in range(NCH):
                ins = P.tensor.transpose(pt[0][:, 128 * j:128 * (j + 1)],
                                         na_tm[b][:, 128 * j:128 * (j + 1)], idb[:])
                if j == NCH - 1:
                    ins.then_inc(*_i(cn, "pe", 1))
            pe_na = N["pe"]
            for j in range(NCH):
                ins = P.tensor.transpose(pt[1][:, 128 * j:128 * (j + 1)],
                                         nf_tm[b][:, 128 * j:128 * (j + 1)], idb[:])
                if j == NCH - 1:
                    ins.then_inc(*_i(cn, "pe", 1))
            pe_done[g] = N["pe"]
            naf_v = naf_sb.ap().rearrange("p (j t) -> p j t", j=NCH)
            nff_v = nff_sb.ap().rearrange("p (j t) -> p j t", j=NCH)
            pt0_v = pt[0].ap().rearrange("p (j t) -> p j t", j=NCH)
            pt1_v = pt[1].ap().rearrange("p (j t) -> p j t", j=NCH)
            P.scalar.wait_ge(SEM["pe"], pe_na)
            P.scalar.activation(naf_v[:, :, 128 * i:128 * (i + 1)], pt0_v[:, :, :],
                                AF.Copy)
            P.scalar.wait_ge(SEM["pe"], pe_done[g])
            P.scalar.activation(nff_v[:, :, 128 * i:128 * (i + 1)], pt1_v[:, :, :],
                                AF.Copy).then_inc(*_i(cn, "ac", 1))
            ac_evac[g] = N["ac"]
        ac_seq = N["ac"]
        hbm_na = na_fm[s].rearrange("(j p) t -> p j t", p=128)
        hbm_nf = nf_fm[s].rearrange("(j p) t -> p j t", p=128)
        sb_na = naf_sb.ap().rearrange("p (j t) -> p j t", j=NCH)
        sb_nf = nff_sb.ap().rearrange("p (j t) -> p j t", j=NCH)
        P.scalar.wait_ge(SEM["ac"], ac_seq)
        P.scalar.dma_start(out=hbm_na, in_=sb_na).then_inc(*_i(cn, "gf", 16))
        P.scalar.dma_start(out=hbm_nf, in_=sb_nf).then_inc(*_i(cn, "gf", 16))
        gd_fm = N["gf"]
        if s + 1 < SEQ:
            P.scalar.wait_ge(SEM["gf"], gd_fm)

    st.close()


def emit_l1_scan(nc, P, cn, dt):
    na_fm, hx_fm = dt["na_fm"], dt["hx_fm"]
    wgt, wct, lnw = dt["wgtb"], dt["wctb"], dt["ln_w"]

    st = contextlib.ExitStack()
    sb = lambda nm, *a: st.enter_context(nc.sbuf_tensor("E_" + nm, *a))
    ps = lambda nm, *a: st.enter_context(nc.psum_tensor("E_" + nm, *a))
    naf = sb("naf", [128, NCH * T], BF16)
    wgb = sb("wgb", [128, NCH * C], BF16)
    wcb = sb("wcb", [128, NCH * C], BF16)
    lnwb = sb("lnwb", [128, NCH], F32)
    a_sb = sb("a_sb", [128, T], F32)
    xsc = sb("xsc", [128, T], F32)
    beta = [sb(f"beta{n}", [128, 512], F32) for n in range(2)]
    hxc = sb("hxc", [128, T], F32)
    pb = [ps(f"pb{n}", [128, 512], F32) for n in range(4)]

    SEM, N = cn.sem, cn.n
    P.sync.dma_start(out=wgb.ap().rearrange("p (j c) -> p j c", j=NCH),
                     in_=wgt.rearrange("(j p) c -> p j c", p=128)) \
        .then_inc(*_i(cn, "lw", 16))
    P.sync.dma_start(out=wcb.ap().rearrange("p (j c) -> p j c", j=NCH),
                     in_=wct.rearrange("(j p) c -> p j c", p=128)) \
        .then_inc(*_i(cn, "lw", 16))
    lw_all = N["lw"]
    P.sync.dma_start(out=lnwb[:], in_=lnw[:]).then_inc(*_i(cn, "lg", 16))
    lg_all = N["lg"]
    P.vector.wait_ge(SEM["lw"], lw_all)
    P.vector.wait_ge(SEM["lg"], lg_all)
    for dj in range(NCH):
        P.vector.tensor_scalar(wgb[:, dj * C:(dj + 1) * C],
                               wgb[:, dj * C:(dj + 1) * C],
                               lnwb[:, dj:dj + 1], None, ALU.mult)
        ins = P.vector.tensor_scalar(wcb[:, dj * C:(dj + 1) * C],
                                     wcb[:, dj * C:(dj + 1) * C],
                                     lnwb[:, dj:dj + 1], None, ALU.mult)
    ins.then_inc(*_i(cn, "dv", 1))
    dv_fold = N["dv"]
    P.tensor.wait_ge(SEM["dv"], dv_fold)

    dv_scan_read = 0
    sh_prev = 0
    nu_c = [0]
    dv_xsc = {-1: 0, -2: 0}
    for s in range(SEQ):
        if s > 0:
            P.sync.wait_ge(SEM["pe"], pe_seq_last)
        P.sync.dma_start(out=naf.ap().rearrange("p (j t) -> p j t", j=NCH),
                         in_=na_fm[s].rearrange("(j p) t -> p j t", p=128)) \
            .then_inc(*_i(cn, "lx", 16))
        lx_t = N["lx"]
        P.tensor.wait_ge(SEM["lx"], lx_t)
        for j in range(NCH):
            for nt in range(4):
                nu = nu_c[0]; nu_c[0] += 1
                p1, p2 = pb[(nu % 2) * 2], pb[(nu % 2) * 2 + 1]
                P.tensor.wait_ge(SEM["dv"], dv_xsc[nu - 2])
                for dj in range(NCH):
                    ins = P.tensor.matmul(
                        p1[:], wgb[:, dj * C + j * 128: dj * C + (j + 1) * 128],
                        naf[:, dj * T + nt * 512: dj * T + (nt + 1) * 512],
                        start=(dj == 0), stop=(dj == NCH - 1))
                ins.then_inc(*_i(cn, "pe", 1))
                for dj in range(NCH):
                    ins = P.tensor.matmul(
                        p2[:], wcb[:, dj * C + j * 128: dj * C + (j + 1) * 128],
                        naf[:, dj * T + nt * 512: dj * T + (nt + 1) * 512],
                        start=(dj == 0), stop=(dj == NCH - 1))
                ins.then_inc(*_i(cn, "pe", 1))
                pe_p = N["pe"]
                P.scalar.wait_ge(SEM["pe"], pe_p)
                P.scalar.wait_ge(SEM["dv"], dv_xsc[nu - 2])
                if nt == 0:
                    P.scalar.wait_ge(SEM["dv"], dv_scan_read)
                ts_ = slice(nt * 512, (nt + 1) * 512)
                P.scalar.activation(a_sb[:, ts_], p1[:], AF.Sigmoid, scale=-1.0)
                P.scalar.activation(beta[nu % 2][:], p1[:], AF.Sigmoid) \
                    .then_inc(*_i(cn, "ac", 1))
                ac_b = N["ac"]
                P.vector.wait_ge(SEM["ac"], ac_b)
                P.vector.tensor_tensor(xsc[:, ts_], beta[nu % 2][:], p2[:],
                                       ALU.mult).then_inc(*_i(cn, "dv", 1))
                dv_xsc[nu] = N["dv"]
            P.vector.drain()
            P.vector.tensor_tensor_scan(hxc[:], a_sb[:], xsc[:], 0.0,
                                        ALU.mult, ALU.add) \
                .then_inc(*_i(cn, "dv", 1))
            dv_scan_read = N["dv"]
            P.scalar.wait_ge(SEM["dv"], dv_scan_read)
            P.scalar.dma_start(out=hx_fm[s, j * 128:(j + 1) * 128, :], in_=hxc[:]) \
                .then_inc(*_i(cn, "sh", 16))
            sh_prev = N["sh"]
            P.vector.wait_ge(SEM["sh"], sh_prev)
        pe_seq_last = N["pe"]

    st.close()


def emit_l1_rest(nc, P, cn, dt):
    hx_fm, nf_fm, x1, y = dt["hx_fm"], dt["nf_fm"], dt["x1"], dt["y"]
    wts = {nm: dt[nm + "b"] for nm in ("w1t", "w2t", "w3t", "wrt", "wkt",
                                       "wvt2")}
    ffnw, mur, muk = dt["ffn_w"], dt["mu_r"], dt["mu_k"]
    ident, onescol, onesrow = dt["ident"], dt["ones_col"], dt["ones_row"]

    st = contextlib.ExitStack()
    sb = lambda nm, *a: st.enter_context(nc.sbuf_tensor("F_" + nm, *a))
    ps = lambda nm, *a: st.enter_context(nc.psum_tensor("F_" + nm, *a))
    hxs = sb("hxs", [128, NCH * T], F32)
    buf_hk = sb("buf_hk", [128, NCH * T], BF16)
    buf_gk = sb("buf_gk", [128, NCH * T], BF16)
    hnb, xkb = buf_hk, buf_hk
    glb, kab = buf_gk, buf_gk
    xrb = sb("xrb", [128, NCH * T], BF16)
    nfc = sb("nfc", [128, T], BF16)
    wta = [sb(f"wta{n}", [128, C], BF16) for n in range(2)]
    wtb = [sb(f"wtb{n}", [128, C], BF16) for n in range(2)]
    B3f = sb("B3f", [128, T], F32)
    B3 = B3f
    x1t = B3f[:, 0:C]
    outt = B3f[:, C:2 * C]
    tmu = sb("tmu", [128, T], BF16)
    sqs = [sb(f"sqs{n}", [128, 512], F32) for n in range(2)]
    s1b = [sb(f"s1b{n}", [128, 512], BF16) for n in range(2)]
    tmax = [sb(f"tmax{n}", [128, 512], F32) for n in range(2)]
    rb2 = [sb(f"rb2{n}", [128, 512], F32) for n in range(2)]
    t2b = [sb(f"t2b{n}", [128, 512], F32) for n in range(2)]
    msq = sb("msq", [1, 512], F32)
    rms3 = sb("rms3", [1, 512], F32)
    cvec = {nm: sb("c_" + nm, [128, NCH], F32)
            for nm in ("ffnw", "mur", "muk", "omr", "omk")}
    onesc = sb("onesc", [128, 1], F32)
    onesr = sb("onesr", [1, 128], F32)
    idf = sb("idf", [128, 128], F32)
    pA = ps("pA", [128, 512], F32)
    pB = ps("pB", [128, 512], F32)
    pms = ps("pms", [1, 512], F32)
    pbc = ps("pbc", [128, 512], F32)
    ptf = [ps(f"ptf{n}", [128, 512], F32) for n in range(2)]

    SEM, N = cn.sem, cn.n

    for src_, dst in ((ffnw, cvec["ffnw"]), (mur, cvec["mur"]), (muk, cvec["muk"]),
                      (ident, idf), (onescol, onesc), (onesrow, onesr)):
        P.sync.dma_start(out=dst[:], in_=src_[:]).then_inc(*_i(cn, "lc", 16))
    lc_all = N["lc"]
    P.vector.wait_ge(SEM["lc"], lc_all)
    P.scalar.wait_ge(SEM["lc"], lc_all)
    P.tensor.wait_ge(SEM["lc"], lc_all)
    P.vector.tensor_scalar(cvec["omr"][:], cvec["mur"][:], -1.0, 1.0,
                           ALU.mult, ALU.add)
    P.vector.tensor_scalar(cvec["omk"][:], cvec["muk"][:], -1.0, 1.0,
                           ALU.mult, ALU.add)
    P.vector.drain()

    wau = [0]; wa_free = {-1: 0, -2: 0}
    wbu = [0]; wb_free = {-1: 0, -2: 0}
    nfu = [0]; nf_free = {-1: 0, -2: 0}
    dv_fold = {}

    def load_wchunk(bufs, use_c, free_d, sems, wnm, j, fold):
        u = use_c[0]; use_c[0] += 1
        par = u % 2
        lws = sems[par]
        P.sync.wait_ge(SEM["pe"], free_d[u - 2])
        s3 = wts[wnm].rearrange("(dj p) c -> p dj c", p=128)[
            :, :, j * 128:(j + 1) * 128]
        d3 = bufs[par].ap().rearrange("p (dj c) -> p dj c", dj=NCH)
        P.sync.dma_start(out=d3, in_=s3).then_inc(*_i(cn, lws, 16))
        lw_t = N[lws]
        if fold:
            P.vector.wait_ge(SEM[lws], lw_t)
            for dj in range(NCH):
                ins = P.vector.tensor_scalar(
                    bufs[par][:, dj * 128:(dj + 1) * 128],
                    bufs[par][:, dj * 128:(dj + 1) * 128],
                    cvec["ffnw"][:, dj:dj + 1], None, ALU.mult)
            ins.then_inc(*_i(cn, "dv", 1))
            return par, lw_t, N["dv"]
        return par, lw_t, None

    sy_done = 0
    dv_out_read = 0
    squ = [0]
    pe_sq = {}
    pe_seq_end = 0
    dv_hxs_free = 0
    for s in range(SEQ):
        if s > 0:
            P.sync.wait_ge(SEM["dv"], dv_hxs_free)
            P.sync.wait_ge(SEM["pe"], pe_seq_end)
        P.sync.dma_start(out=hxs.ap().rearrange("p (j t) -> p j t", j=NCH),
                         in_=hx_fm[s].rearrange("(j p) t -> p j t", p=128)) \
            .then_inc(*_i(cn, "lh", 16))
        lh_t = N["lh"]
        P.vector.wait_ge(SEM["lh"], lh_t)
        P.scalar.wait_ge(SEM["lh"], lh_t)
        if s > 0:
            P.scalar.wait_ge(SEM["pe"], pe_seq_end)
            P.scalar.wait_ge(SEM["sy"], sy_done)
        for nt in range(4):
            ts_ = slice(nt * 512, (nt + 1) * 512)
            for j in range(NCH):
                u = squ[0]; squ[0] += 1
                if (u - 2) in pe_sq:
                    P.scalar.wait_ge(SEM["pe"], pe_sq[u - 2])
                P.scalar.activation(sqs[u % 2][:], hxs[:, j * T + nt * 512:
                                                       j * T + (nt + 1) * 512],
                                    AF.Square).then_inc(*_i(cn, "ac", 1))
                P.tensor.wait_ge(SEM["ac"], N["ac"])
                ins = P.tensor.matmul(pms[:], onesc[:], sqs[u % 2][:],
                                      start=(j == 0), stop=(j == NCH - 1))
                ins.then_inc(*_i(cn, "pe", 1))
                pe_sq[u] = N["pe"]
            pe_ms = N["pe"]
            P.vector.wait_ge(SEM["pe"], pe_ms)
            P.vector.drain()
            P.vector.tensor_scalar(msq[:], pms[:], 1.0 / C, EPS_RMS1,
                                   ALU.mult, ALU.add)
            P.vector.drain()
            P.vector.reciprocal(rms3[:], msq[:]).then_inc(*_i(cn, "dv", 1))
            dv_ms = N["dv"]
            P.tensor.wait_ge(SEM["dv"], dv_ms)
            P.scalar.wait_ge(SEM["dv"], dv_ms)
            P.scalar.activation(rms3[:], rms3[:], AF.Sqrt) \
                .then_inc(*_i(cn, "ac", 1))
            ac_r3 = N["ac"]
            P.tensor.wait_ge(SEM["ac"], ac_r3)
            if nt >= 1:
                P.tensor.wait_ge(SEM["ac"], ac_bc)
            ins = P.tensor.matmul(pbc[:], onesr[:], rms3[:], start=True, stop=True)
            ins.then_inc(*_i(cn, "pe", 1))
            pe_bc = N["pe"]
            P.scalar.wait_ge(SEM["pe"], pe_bc)
            P.scalar.activation(B3[:, ts_], pbc[:], AF.Copy) \
                .then_inc(*_i(cn, "ac", 1))
            ac_bc = N["ac"]
        ac_b3_done = N["ac"]
        P.vector.wait_ge(SEM["ac"], ac_b3_done)
        if s > 0:
            P.vector.wait_ge(SEM["pe"], pe_seq_end)
        for j in range(NCH):
            for nt in range(4):
                ins = P.vector.tensor_tensor(
                    hnb[:, j * T + nt * 512: j * T + (nt + 1) * 512],
                    hxs[:, j * T + nt * 512: j * T + (nt + 1) * 512],
                    B3[:, nt * 512:(nt + 1) * 512], ALU.mult)
        ins.then_inc(*_i(cn, "dv", 1))
        dv_hn = N["dv"]
        dv_gl = {}
        gu = [0]
        for j in range(NCH):
            para, lwa_t, dva = load_wchunk(wta, wau, wa_free, ("lwa0", "lwa1"),
                                           "w1t", j, True)
            parb, lwb_t, dvb = load_wchunk(wtb, wbu, wb_free, ("lwb0", "lwb1"),
                                           "w3t", j, True)
            P.tensor.wait_ge(SEM["dv"], max(dva, dvb, dv_hn))
            for nt in range(4):
                u = gu[0]; gu[0] += 1
                P.tensor.wait_ge(SEM["dv"], dv_gl.get(u - 1, 0))
                for dj in range(NCH):
                    ins = P.tensor.matmul(
                        pA[:], wta[para][:, dj * 128:(dj + 1) * 128],
                        hnb[:, dj * T + nt * 512: dj * T + (nt + 1) * 512],
                        start=(dj == 0), stop=(dj == NCH - 1))
                ins.then_inc(*_i(cn, "pe", 1))
                pe_1 = N["pe"]
                for dj in range(NCH):
                    ins = P.tensor.matmul(
                        pB[:], wtb[parb][:, dj * 128:(dj + 1) * 128],
                        hnb[:, dj * T + nt * 512: dj * T + (nt + 1) * 512],
                        start=(dj == 0), stop=(dj == NCH - 1))
                ins.then_inc(*_i(cn, "pe", 1))
                pe_3 = N["pe"]
                P.scalar.wait_ge(SEM["pe"], pe_1)
                if (u - 2) in dv_gl:
                    P.scalar.wait_ge(SEM["dv"], dv_gl[u - 2])
                P.scalar.activation(s1b[u % 2][:], pA[:], AF.Silu) \
                    .then_inc(*_i(cn, "ac", 1))
                ac_s1 = N["ac"]
                P.vector.wait_ge(SEM["ac"], ac_s1)
                P.vector.wait_ge(SEM["pe"], pe_3)
                P.vector.tensor_tensor(
                    glb[:, j * T + nt * 512: j * T + (nt + 1) * 512],
                    pB[:], s1b[u % 2][:], ALU.mult).then_inc(*_i(cn, "dv", 1))
                dv_gl[u] = N["dv"]
            wa_free[wau[0] - 1] = N["pe"]
            wb_free[wbu[0] - 1] = N["pe"]
        dv_gl_done = N["dv"]
        P.tensor.wait_ge(SEM["dv"], dv_gl_done)
        dv_s = {}
        su_ = [0]
        for dj in range(NCH):
            para, lwa_t, _ = load_wchunk(wta, wau, wa_free, ("lwa0", "lwa1"),
                                         "w2t", dj, False)
            P.tensor.wait_ge(SEM["lwa0" if para == 0 else "lwa1"], lwa_t)
            for nt in range(4):
                u = su_[0]; su_[0] += 1
                P.tensor.wait_ge(SEM["dv"], dv_s.get(u - 1, 0))
                for j in range(NCH):
                    ins = P.tensor.matmul(
                        pA[:], wta[para][:, j * 128:(j + 1) * 128],
                        glb[:, j * T + nt * 512: j * T + (nt + 1) * 512],
                        start=(j == 0), stop=(j == NCH - 1))
                ins.then_inc(*_i(cn, "pe", 1))
                pe_f = N["pe"]
                P.vector.wait_ge(SEM["pe"], pe_f)
                hsl = hxs[:, dj * T + nt * 512: dj * T + (nt + 1) * 512]
                P.vector.tensor_tensor(hsl, hsl, pA[:], ALU.add) \
                    .then_inc(*_i(cn, "dv", 1))
                dv_s[u] = N["dv"]
            wa_free[wau[0] - 1] = N["pe"]
        dv_s_done = N["dv"]
        pe_ff_end = N["pe"]
        P.vector.wait_ge(SEM["pe"], pe_ff_end)
        dv_nf = 0
        for j in range(NCH):
            u = nfu[0]; nfu[0] += 1
            lns = "ln0" if u % 2 == 0 else "ln1"
            P.sync.wait_ge(SEM["dv"], nf_free[u - 1])
            P.sync.dma_start(out=nfc[:],
                             in_=nf_fm[s, j * 128:(j + 1) * 128, :]) \
                .then_inc(*_i(cn, lns, 16))
            ln_t = N[lns]
            P.vector.wait_ge(SEM[lns], ln_t)
            for mu_c, om_c, dstb in ((cvec["mur"], cvec["omr"], xrb),
                                     (cvec["muk"], cvec["omk"], xkb)):
                P.vector.drain()
                P.vector.tensor_scalar(tmu[:], nfc[:], mu_c[:, j:j + 1],
                                       None, ALU.mult)
                P.vector.drain()
                P.vector.scalar_tensor_tensor(
                    dstb[:, j * T + 1:(j + 1) * T], nfc[:, 0:T - 1],
                    om_c[:, j:j + 1], tmu[:, 1:T], ALU.mult, ALU.add)
                ins = P.vector.tensor_copy(dstb[:, j * T:j * T + 1], tmu[:, 0:1])
            ins.then_inc(*_i(cn, "dv", 1))
            nf_free[u] = N["dv"]
        dv_xrk = N["dv"]
        P.tensor.wait_ge(SEM["dv"], dv_xrk)
        ku = [0]
        dv_ka = {}
        ac_ka = {}
        for dj in range(NCH):
            para, lwa_t, _ = load_wchunk(wta, wau, wa_free, ("lwa0", "lwa1"),
                                         "wkt", dj, False)
            P.tensor.wait_ge(SEM["lwa0" if para == 0 else "lwa1"], lwa_t)
            for nt in range(4):
                u = ku[0]; ku[0] += 1
                P.tensor.wait_ge(SEM["dv"], dv_ka.get(u - 1, 0))
                for j in range(NCH):
                    ins = P.tensor.matmul(
                        pA[:], wta[para][:, j * 128:(j + 1) * 128],
                        xkb[:, j * T + nt * 512: j * T + (nt + 1) * 512],
                        start=(j == 0), stop=(j == NCH - 1))
                ins.then_inc(*_i(cn, "pe", 1))
                pe_k = N["pe"]
                P.vector.wait_ge(SEM["pe"], pe_k)
                if (u - 2) in ac_ka:
                    P.vector.wait_ge(SEM["ac"], ac_ka[u - 2])
                P.vector.tensor_scalar(tmax[u % 2][:], pA[:], 0.0, None,
                                       ALU.max).then_inc(*_i(cn, "dv", 1))
                dv_ka[u] = N["dv"]
                if u == 0:
                    P.scalar.wait_ge(SEM["pe"], pe_ff_end)
                P.scalar.wait_ge(SEM["dv"], dv_ka[u])
                P.scalar.activation(kab[:, dj * T + nt * 512:
                                        dj * T + (nt + 1) * 512],
                                    tmax[u % 2][:], AF.Square) \
                    .then_inc(*_i(cn, "ac", 1))
                ac_ka[u] = N["ac"]
            wa_free[wau[0] - 1] = N["pe"]
        ac_ka_done = N["ac"]
        P.tensor.wait_ge(SEM["ac"], ac_ka_done)
        vu = [0]
        dv_v = {}
        ac_r = {}
        for dj in range(NCH):
            para, lwa_t, _ = load_wchunk(wta, wau, wa_free, ("lwa0", "lwa1"),
                                         "wrt", dj, False)
            parb, lwb_t, _ = load_wchunk(wtb, wbu, wb_free, ("lwb0", "lwb1"),
                                         "wvt2", dj, False)
            P.tensor.wait_ge(SEM["lwa0" if para == 0 else "lwa1"], lwa_t)
            P.tensor.wait_ge(SEM["lwb0" if parb == 0 else "lwb1"], lwb_t)
            for nt in range(4):
                u = vu[0]; vu[0] += 1
                P.tensor.wait_ge(SEM["dv"], dv_v.get(u - 1, 0))
                for j in range(NCH):
                    ins = P.tensor.matmul(
                        pA[:], wta[para][:, j * 128:(j + 1) * 128],
                        xrb[:, j * T + nt * 512: j * T + (nt + 1) * 512],
                        start=(j == 0), stop=(j == NCH - 1))
                ins.then_inc(*_i(cn, "pe", 1))
                pe_r = N["pe"]
                for j in range(NCH):
                    ins = P.tensor.matmul(
                        pB[:], wtb[parb][:, j * 128:(j + 1) * 128],
                        kab[:, j * T + nt * 512: j * T + (nt + 1) * 512],
                        start=(j == 0), stop=(j == NCH - 1))
                ins.then_inc(*_i(cn, "pe", 1))
                pe_v = N["pe"]
                P.scalar.wait_ge(SEM["pe"], pe_r)
                if (u - 2) in dv_v:
                    P.scalar.wait_ge(SEM["dv"], dv_v[u - 2])
                P.scalar.activation(rb2[u % 2][:], pA[:], AF.Sigmoid) \
                    .then_inc(*_i(cn, "ac", 1))
                ac_r[u] = N["ac"]
                P.vector.wait_ge(SEM["ac"], ac_r[u])
                P.vector.wait_ge(SEM["pe"], pe_v)
                P.vector.tensor_tensor(t2b[u % 2][:], pB[:], rb2[u % 2][:],
                                       ALU.mult)
                P.vector.drain()
                hsl = hxs[:, dj * T + nt * 512: dj * T + (nt + 1) * 512]
                P.vector.tensor_tensor(hsl, hsl, t2b[u % 2][:], ALU.add) \
                    .then_inc(*_i(cn, "dv", 1))
                dv_v[u] = N["dv"]
            wa_free[wau[0] - 1] = N["pe"]
            wb_free[wbu[0] - 1] = N["pe"]
        dv_final_s = N["dv"]
        dv_hxs_free = dv_final_s
        P.tensor.wait_ge(SEM["dv"], dv_final_s)
        dv_out_read = max(dv_out_read, dv_final_s)
        for i in range(NTT):
            P.sync.wait_ge(SEM["dv"], dv_out_read)
            P.sync.dma_start(out=x1t, in_=x1[s, 128 * i:128 * (i + 1), :]) \
                .then_inc(*_i(cn, "lx1", 16))
            lx1_t = N["lx1"]
            if i > 0:
                P.tensor.wait_ge(SEM["dv"], dv_out)
            for half in range(2):
                for djj in range(4):
                    dj = half * 4 + djj
                    ins = P.tensor.transpose(
                        ptf[half][:, djj * 128:(djj + 1) * 128],
                        hxs[:, dj * T + 128 * i: dj * T + 128 * (i + 1)], idf[:])
                    if djj == 3:
                        ins.then_inc(*_i(cn, "pe", 1))
            pe_tp = N["pe"]
            P.vector.wait_ge(SEM["pe"], pe_tp)
            P.vector.wait_ge(SEM["lx1"], lx1_t)
            P.vector.wait_ge(SEM["sy"], sy_done)
            P.vector.tensor_tensor(outt[:, 0:512], x1t[:, 0:512], ptf[0][:],
                                   ALU.add)
            P.vector.tensor_tensor(outt[:, 512:1024], x1t[:, 512:1024], ptf[1][:],
                                   ALU.add).then_inc(*_i(cn, "dv", 1))
            dv_out = N["dv"]
            dv_out_read = dv_out
            P.scalar.wait_ge(SEM["dv"], dv_out)
            P.scalar.dma_start(out=y[s, 128 * i:128 * (i + 1), :], in_=outt) \
                .then_inc(*_i(cn, "sy", 16))
            sy_done = N["sy"]
        pe_seq_end = N["pe"]

    st.close()


def _i(cn, nm, v):
    cn.n[nm] += v
    return cn.sem[nm], v


def _run(nc, in_maps, **kw):
    return run_bass_kernel_spmd(nc, in_maps, core_ids=list(range(len(in_maps))), **kw)


ALL_SEMS = ["pre0", "preSq", "preB", "pre1",
            "ld0", "ld1", "gc", "ga0", "ga1", "gw", "gf",
            "lc", "lx", "lw0", "lw1", "la0", "la1", "sa0", "sa1",
            "lb", "la", "lg", "lv0", "lv1", "sa", "lw", "sh",
            "lwa0", "lwa1", "lwb0", "lwb1", "lh", "ln0", "ln1", "lx1", "sy", "lz",
            "dv", "ac", "pe"]

STORE_SEMS = ["ga0", "ga1", "gw", "gf", "sa0", "sa1", "sa", "sh", "sy"]


def _declare_drams(nc):
    dt = {}
    EI, IN, EO = "ExternalInput", "Internal", "ExternalOutput"

    def d(name, shape, dtype, kind):
        dt[name] = nc.dram_tensor(name, shape, dtype, kind=kind).ap()
    d("x", [SEQ, T, C], F32, EI)
    d("cent", [128, NCH * ER], F32, EI)
    d("ident", [128, 128], F32, EI)
    d("rope_cs", [128, T], BF16, EI)
    d("rope_sn", [128, T], BF16, EI)
    d("perm_swap", [128, 128], BF16, EI)
    for nm in ("wq1L", "wk1L", "wq2L", "wk2L"):
        d(nm, [8, 128, NCH * 128], BF16, IN)
    d("wvL", [2, 128, NCH * 512], BF16, IN)
    d("masks", [4, 128, 512], BF16, EI)
    d("tri128", [128, 128], BF16, EI)
    for nm in SQUARE_WTS:
        d(nm, [C, C], F32, EI)
        d(nm + "b", [C, C], BF16, IN)
    d("w1s", [ES, C, HF], F32, EI)
    d("w2s", [ES, HF, C], F32, EI)
    d("w1r", [ER, C, HF], F32, EI)
    d("w2r", [ER, HF, C], F32, EI)
    d("w1Ls", [ES, 8, 128, NCH * 512], BF16, IN)
    d("w2Ls", [ES, 8, 128, 4 * C], BF16, IN)
    d("w1Lr", [ER, 8, 128, NCH * 512], BF16, IN)
    d("w2Lr", [ER, 8, 128, 4 * C], BF16, IN)
    d("b1s", [128, ES * 32], F32, EI)
    d("b1r", [128, ER * 32], F32, EI)
    d("b2s", [1, ES * C], F32, EI)
    d("b2r", [1, ER * C], F32, EI)
    d("b2sbb", [1, ES * C], BF16, IN)
    d("b2rbb", [1, ER * C], BF16, IN)
    for nm in ("g_s", "g_r", "ln_w", "ffn_w", "mu_r", "mu_k"):
        d(nm, [128, NCH], F32, EI)
    d("ones_row", [1, 128], F32, EI)
    d("ones_col", [128, 1], F32, EI)
    d("d_onesb", [1, 128], BF16, IN)
    d("d_identb", [128, 128], BF16, IN)
    d("d_centb", [128, NCH * ER], BF16, IN)
    d("xn_fm", [SEQ, C, T], BF16, IN)
    d("xnr_fm", [SEQ, C, T], BF16, IN)
    d("acc0", [SEQ, T, C], F32, IN)
    d("acc1", [SEQ, T, C], F32, IN)
    d("wg", [SEQ, T, ER], F32, IN)
    d("x1", [SEQ, T, C], F32, IN)
    d("na_fm", [SEQ, C, T], BF16, IN)
    d("nf_fm", [SEQ, C, T], BF16, IN)
    d("hx_fm", [SEQ, C, T], F32, IN)
    d("y", [SEQ, T, C], F32, EO)
    if _DBG:
        d("dbg_y", [SEQ, 4, 128, 8 * 512], BF16, EO)
        d("dbg_k1", [SEQ, 128, 8 * T], BF16, EO)
        d("dbg_k2", [SEQ, 128, 8 * T], BF16, EO)
        d("dbg_v", [SEQ, 128, NTT * C], BF16, EO)
        d("dbg_q", [8, 128, 1024], BF16, EO)
    return dt


SQUARE_WTS = ("wq1t", "wk1t", "wq2t", "wk2t", "wvt", "woutt",
              "wgt", "wct", "w1t", "w2t", "w3t", "wrt", "wkt", "wvt2")


def emit_prestage(nc, P, cn, dt):
    for src_, dst in (("ident", "d_identb"), ("cent", "d_centb"),
                      ("ones_row", "d_onesb"), ("b2s", "b2sbb"), ("b2r", "b2rbb")):
        P.gpsimd.dma_start(out=dt[dst], in_=dt[src_]).then_inc(*_i(cn, "pre0", 16))
    for nm in SQUARE_WTS:
        if nm in ("wq1t", "wk1t", "wq2t", "wk2t", "wvt", "woutt"):
            continue
        P.gpsimd.dma_start(out=dt[nm + "b"], in_=dt[nm]) \
            .then_inc(*_i(cn, "preSq", 16))
    for ne, w1f, w2f, w1d, w2d in ((ES, "w1s", "w2s", "w1Ls", "w2Ls"),
                                   (ER, "w1r", "w2r", "w1Lr", "w2Lr")):
        for e in range(ne):
            w1v = dt[w1f][e].rearrange("(j p) (hb hl) -> hb p j hl",
                                       p=128, hl=512)
            w2v = dt[w2f][e].rearrange("(g q p) c -> g p q c", q=4, p=128)
            for hb in range(8):
                d1 = dt[w1d][e, hb].rearrange("p (j hl) -> p j hl", hl=512)
                P.gpsimd.dma_start(out=d1, in_=w1v[hb]) \
                    .then_inc(*_i(cn, "pre1", 16))
                d2 = dt[w2d][e, hb].rearrange("p (q c) -> p q c", c=C)
                P.gpsimd.dma_start(out=d2, in_=w2v[hb]) \
                    .then_inc(*_i(cn, "pre1", 16))
    P.sync.wait_ge(cn.sem["pre0"], cn.n["pre0"])


def emit_preB(nc, P, cn, dt):
    st = contextlib.ExitStack()
    sb = lambda nm, *a: st.enter_context(nc.sbuf_tensor("PB_" + nm, *a))
    f32t = [sb(f"f{n}", [128, 4096], F32) for n in range(4)]
    bf16t = [sb(f"b{n}", [128, 4096], BF16) for n in range(4)]
    SEM, N = cn.sem, cn.n
    for Q in (P.sync, P.scalar):
        for nm in ("dv", "ac", "pe", "ld0", "ld1", "ga0", "ga1", "gw", "gf"):
            Q.wait_ge(SEM[nm], N[nm])
    jobs = []
    for srcn, dstn in (("wk1t", "wk1L"), ("wk2t", "wk2L"),
                       ("wq1t", "wq1L"), ("wq2t", "wq2L")):
        v = dt[srcn].rearrange("(cj p) (hp c) -> p hp cj c", p=128, c=128)
        for h4 in range(2):
            dst = dt[dstn][h4 * 4:(h4 + 1) * 4].rearrange("hp p f -> p hp f")
            jobs.append(([(v[:, h4 * 4 + hh], 8, 128) for hh in range(4)],
                         dst))
    vv = dt["wvt"].rearrange("(cj p) (ha c) -> p ha cj c", p=128, c=512)
    for ha in range(2):
        dst = dt["wvL"][ha:ha + 1].rearrange("ha p f -> p ha f")
        jobs.append(([(vv[:, ha], 8, 512)], dst))
    wo = dt["woutt"].rearrange("(hp p) c -> p hp c", p=128)
    wob = dt["wouttb"].rearrange("(hp p) c -> p hp c", p=128)
    for h4 in range(2):
        jobs.append(([(wo[:, h4 * 4 + hh], 1, 1024) for hh in range(4)],
                     wob[:, h4 * 4:(h4 + 1) * 4]))
    ac_done = {}
    stq = {}
    ld_done = {}
    scr = sb("scr", [128, 512], BF16)
    nj = len(jobs)
    def issue_load(i):
        srcs, dst3 = jobs[i]
        par = i % 4
        if i - 4 >= 0:
            P.sync.wait_ge(SEM["preB"], stq[i - 4])
        w = 4096 // len(srcs)
        for si, (s3, jj, cc) in enumerate(srcs):
            fv = f32t[par][:, si * w:(si + 1) * w] \
                .rearrange("p (j c) -> p j c", j=jj)
            P.sync.dma_start(out=fv, in_=s3).then_inc(*_i(cn, "lc", 16))
        ld_done[i] = N["lc"]

    issue_load(0)
    issue_load(1)
    for i in range(nj):
        srcs, dst3 = jobs[i]
        par = i % 4
        a = dst3.shape[1]
        bv = bf16t[par].ap().rearrange("p (a f) -> p a f", a=a)
        if i + 2 < nj:
            issue_load(i + 2)
        else:
            P.sync.dma_start(out=f32t[(i + 2) % 4][:, 0:64],
                             in_=dt["tri128"].bitcast(F32)[:, 0:64]) \
                .then_inc(*_i(cn, "lc", 16))
            ld_done[i + 2] = N["lc"]
        P.scalar.wait_ge(SEM["lc"], ld_done[i + 1])
        P.scalar.activation(bf16t[par][:], f32t[par][:], AF.Copy) \
            .then_inc(*_i(cn, "ac", 1))
        P.scalar.activation(scr[:], bf16t[par][:, 0:512], AF.Copy) \
            .then_inc(*_i(cn, "ac", 1))
        ac_done[i] = N["ac"]
        P.scalar.wait_ge(SEM["ac"], ac_done[i])
        P.scalar.dma_start(out=dst3, in_=bv).then_inc(*_i(cn, "preB", 16))
        stq[i] = N["preB"]
    st.close()


def _barrier(P, cn, sems=None):
    names = ["dv", "ac", "pe"] + STORE_SEMS if sems is None else sems
    for eng in (P.sync, P.tensor, P.vector, P.scalar, P.gpsimd):
        for nm in names:
            eng.wait_ge(cn.sem[nm], cn.n[nm])


def build_fused():
    nc = bass.Bass("TRN2", target_bir_lowering=False)
    P = Prog(nc)
    dt = _declare_drams(nc)
    cn = Counters(nc, ALL_SEMS)
    emit_prestage(nc, P, cn, dt)
    emit_l0_prep(nc, P, cn, dt)
    emit_preB(nc, P, cn, dt)
    _barrier(P, cn)
    for eng in (P.sync, P.tensor, P.vector, P.scalar):
        eng.wait_ge(cn.sem["preB"], cn.n["preB"])
    emit_l0_attn(nc, P, cn, dt)
    _barrier(P, cn)
    P.sync.wait_ge(cn.sem["pre1"], cn.n["pre1"])
    emit_l0_ffn(nc, P, cn, dt)
    _barrier(P, cn)
    emit_l1_prep(nc, P, cn, dt)
    _barrier(P, cn)
    emit_l1_scan(nc, P, cn, dt)
    _barrier(P, cn)
    emit_l1_rest(nc, P, cn, dt)
    _barrier(P, cn)
    P.finish()
    return nc


def kernel(**inputs):
    ip = _host_prep(inputs)
    nc = build_fused()
    ims = []
    for c in range(NCORE):
        im = {"x": ip["x"][c]}
        for nm in ("cent", "ident", "rope_cs", "rope_sn", "perm_swap", "masks",
                   "tri128",
                   "w1s", "w2s", "w1r", "w2r", "b1s", "b1r", "b2s", "b2r",
                   "g_s", "g_r", "ln_w", "ffn_w", "mu_r", "mu_k",
                   "ones_row", "ones_col") + SQUARE_WTS:
            im[nm] = ip[nm]
        ims.append(im)
    res = _run(nc, ims)
    return np.concatenate([res.results[c]["y"] for c in range(NCORE)],
                          0).reshape(16, T, C)
